# version 10
# baseline (speedup 1.0000x reference)
import numpy as np
import concourse.bass as bass
import concourse.mybir as mybir
from concourse.bass_utils import run_bass_kernel_spmd

F32 = mybir.dt.float32
F32R = mybir.dt.float32r
BF16 = mybir.dt.bfloat16
U8 = mybir.dt.uint8
AF = mybir.ActivationFunctionType
ALU = mybir.AluOpType
AX = mybir.AxisListType.X

T = 2048
D = 1024
DEBUG_L0A = False
DEBUG_MOE = False
FFN_ONLY = None
NCH = 8
PITCH = 2052
GOFF = 4

COMPUTE_OF = {"T": "T", "V": "V", "S": "S", "P": "G", "DQ": "Q", "DG": "G"}
ENGINE_STREAMS = {"T", "V", "S", "P"}


class Prog:

    def __init__(self, nc):
        self.nc = nc
        self.sem = {k: nc.semaphore("sem" + k).__enter__()
                    for k in ("T", "V", "S", "P")}
        self.cnt = {k: 0 for k in self.sem}
        self.clock = {e: {} for e in ("T", "V", "S", "G", "Q")}
        self.tiles = {}

    def _dma_ch(self, ch):
        key = "D_" + ch
        if key not in self.sem:
            self.sem[key] = self.nc.semaphore(key).__enter__()
            self.cnt[key] = 0
        return key

    def eng(self, e):
        nc = self.nc
        return {"T": nc.tensor, "V": nc.vector, "S": nc.scalar,
                "G": nc.gpsimd, "Q": nc.sync}[e]

    def _tile(self, name):
        t = self.tiles.get(name)
        if t is None:
            t = self.tiles[name] = {"w": None, "r": {}}
        return t

    def _wait(self, cons, milestones):
        eng = self.eng(cons)
        cl = self.clock[cons]
        for (sk, v) in milestones:
            if sk in ENGINE_STREAMS and COMPUTE_OF[sk] == cons:
                continue
            if cl.get(sk, 0) >= v:
                continue
            eng.wait_ge(self.sem[sk], v)
            cl[sk] = v

    def emit(self, stream, build, reads=(), writes=()):
        cons = COMPUTE_OF[stream]
        ms = []
        for ap in reads:
            t = self._tile(ap.tensor.name)
            if t["w"]:
                ms.append(t["w"])
        for ap in writes:
            t = self._tile(ap.tensor.name)
            if t["w"]:
                ms.append(t["w"])
            ms.extend(t["r"].items())
        self._wait(cons, ms)
        inst = build(self.eng(cons))
        self.cnt[stream] += 1
        inst.then_inc(self.sem[stream], 1)
        m = (stream, self.cnt[stream])
        for ap in reads:
            r = self._tile(ap.tensor.name)["r"]
            r[m[0]] = m[1]
        for ap in writes:
            t = self._tile(ap.tensor.name)
            t["w"] = m
            t["r"] = {}
        return m

    def barrier(self):
        snap = dict(self.cnt)
        for cons in ("T", "V", "S", "G", "Q"):
            eng = self.eng(cons)
            cl = self.clock[cons]
            for sk, v in snap.items():
                if v == 0:
                    continue
                if sk in ENGINE_STREAMS and COMPUTE_OF[sk] == cons:
                    continue
                if cl.get(sk, 0) >= v:
                    continue
                eng.wait_ge(self.sem[sk], v)
                cl[sk] = v
        self.tiles = {}

    def dma(self, stream, out, in_, ch=None):
        qcons = COMPUTE_OF[stream]
        if ch is None:
            nm = out.tensor.name
            if "__" in nm:
                ch = nm.split("__")[0]
            else:
                parts = nm.split("_")
                while len(parts) > 1 and parts[-1].isdigit():
                    parts.pop()
                ch = "_".join(parts)
        key = self._dma_ch(ch)
        ms = []
        ti = self._tile(in_.tensor.name)
        if ti["w"]:
            ms.append(ti["w"])
        to = self._tile(out.tensor.name)
        if to["w"]:
            ms.append(to["w"])
        ms.extend(to["r"].items())
        self._wait(qcons, ms)
        inst = self.eng(qcons).dma_start(out, in_)
        self.cnt[key] += 16
        inst.then_inc(self.sem[key], 16)
        m = (key, self.cnt[key])
        self._tile(in_.tensor.name)["r"][key] = self.cnt[key]
        to = self._tile(out.tensor.name)
        to["w"] = m
        to["r"] = {}
        return m

    def mm(self, ps, lhsT, rhs, start, stop):
        return self.emit("T", lambda e: e.matmul(
            ps, lhsT, rhs, start=start, stop=stop, skip_group_check=True),
            reads=[lhsT, rhs], writes=[ps])

    def transp(self, ps, in_, ident):
        return self.emit("T", lambda e: e.transpose(ps, in_, ident),
                         reads=[in_, ident], writes=[ps])

    def tt(self, out, a, b, op, eng="V"):
        return self.emit(eng, lambda e: e.tensor_tensor(out, a, b, op),
                         reads=[a, b], writes=[out])

    def ts(self, out, a, s1, op, s2=None, op2=None, eng="V", accum=None):
        np_ = out.shape[0]
        if isinstance(s1, bass.AP) and s1.shape[0] != np_:
            s1 = s1[0:np_, :]
        if isinstance(s2, bass.AP) and s2.shape[0] != np_:
            s2 = s2[0:np_, :]
        rd = [a] + ([s1] if isinstance(s1, bass.AP) else []) \
            + ([s2] if isinstance(s2, bass.AP) else [])
        wr = [out] + ([accum] if accum is not None else [])
        if s2 is None and op2 is None:
            return self.emit(eng, lambda e: e.tensor_single_scalar(out, a, s1, op),
                             reads=rd, writes=wr)
        return self.emit(eng, lambda e: e.tensor_scalar(
            out, a, s1, s2, op, op2, accum_out=accum), reads=rd, writes=wr)

    def stt(self, out, a, scalar, b, op0, op1, eng="V"):
        if isinstance(scalar, bass.AP) and scalar.shape[0] != out.shape[0]:
            scalar = scalar[0:out.shape[0], :]
        rd = [a, b] + ([scalar] if isinstance(scalar, bass.AP) else [])
        return self.emit(eng, lambda e: e.scalar_tensor_tensor(
            out, a, scalar, b, op0, op1), reads=rd, writes=[out])

    def act(self, out, a, func, bias=None, scale=None, accum=None):
        kw = {}
        rd = [a]
        np_ = out.shape[0]
        if isinstance(bias, bass.AP) and bias.shape[0] != np_:
            bias = bias[0:np_, :]
        if isinstance(scale, bass.AP) and scale.shape[0] != np_:
            scale = scale[0:np_, :]
        if bias is not None:
            kw["bias"] = bias
            if isinstance(bias, bass.AP):
                rd.append(bias)
        if scale is not None:
            kw["scale"] = scale
            if isinstance(scale, bass.AP):
                rd.append(scale)
        wr = [out]
        if accum is not None:
            kw["accum_out"] = accum
            wr.append(accum)
        return self.emit("S", lambda e: e.activation(out, a, func, **kw),
                         reads=rd, writes=wr)

    def cp(self, out, a, eng="V"):
        if eng == "S":
            return self.emit("S", lambda e: e.copy(out, a), reads=[a], writes=[out])
        return self.emit(eng, lambda e: e.tensor_copy(out, a), reads=[a], writes=[out])

    def red(self, out, a, op, negate=False, eng="V"):
        return self.emit(eng, lambda e: e.tensor_reduce(
            out, a, AX, op, negate=negate), reads=[a], writes=[out])

    def recip(self, out, a):
        self.eng("V").wait_ge(self.sem["V"], self.cnt["V"])
        m = self.emit("V", lambda e: e.reciprocal(out, a), reads=[a], writes=[out])
        self.eng("V").wait_ge(self.sem["V"], self.cnt["V"])
        return m

    def scan(self, out, c, v, init, eng="P"):
        rd = [c, v] + ([init] if isinstance(init, bass.AP) else [])
        return self.emit(eng, lambda e: e.tensor_tensor_scan(
            out, c, v, init, ALU.mult, ALU.add), reads=rd, writes=[out])

    def memset(self, ap, val, eng="V"):
        return self.emit(eng, lambda e: e.memset(ap, val), writes=[ap])

    def select(self, out, mask, tru, fls):
        return self.emit("V", lambda e: e.select(out, mask, tru, fls),
                         reads=[mask, tru, fls], writes=[out])


class Arena:

    def __init__(self, nc, base, size):
        self.nc = nc
        self.base = base
        self.size = size
        self.off = 0
        self.n = 0

    def reset(self):
        self.off = 0

    def mark(self):
        return self.off

    def reset_to(self, m):
        self.off = m

    def alloc(self, shape, dtype, tag=""):
        nb = int(np.prod(shape[1:])) * mybir.dt.size(dtype)
        nb = (nb + 31) & ~31
        assert self.off + nb <= self.size, \
            f"arena overflow: {tag} needs {nb} at {self.off}/{self.size}"
        self.n += 1
        h = self.nc.alloc_sbuf_tensor_at(
            f"ar_{tag}__{self.n}", list(shape), dtype, offset=self.base + self.off)
        self.off += nb
        return h.ap()


def cdiv(a, b):
    return (a + b - 1) // b



def _np_consts():
    import ml_dtypes
    bf = ml_dtypes.bfloat16
    C = {}
    C["ident_f32"] = np.eye(128, dtype=np.float32)
    C["ident_bf"] = np.eye(128, dtype=np.float32).astype(bf)

    R = np.zeros((128, 128), dtype=np.float32)
    for base in (0, 64):
        for i in range(32):
            R[base + i, base + i + 32] = -1.0
    RT = np.zeros((128, 128), dtype=np.float32)
    for base in (0, 64):
        for i in range(32):
            RT[base + i + 32, base + i] = -1.0
            RT[base + i, base + i + 32] = 1.0
    C["rl0"] = RT

    RT1 = np.zeros((128, 128), dtype=np.float32)
    for base in (0, 64):
        for i in range(32):
            RT1[base + 2 * i + 1, base + 2 * i] = -1.0
            RT1[base + 2 * i, base + 2 * i + 1] = 1.0
    C["rl1"] = RT1

    half = np.arange(0, 64, 2, dtype=np.float64)
    inv_freq = 1.0 / 10000.0 ** (half / 64.0)
    t_ar = np.arange(T, dtype=np.float64)
    freqs = t_ar[:, None] * inv_freq[None, :]
    freqs = np.concatenate([freqs, freqs], -1)
    base_scale = (half + 0.4 * 64.0) / (1.4 * 64.0)
    power = (t_ar - T // 2) / 128.0
    scale = base_scale[None, :] ** power[:, None]
    scale = np.concatenate([scale, scale], -1)
    cosf, sinf = np.cos(freqs), np.sin(freqs)

    def dup(a):
        return np.tile(np.ascontiguousarray(a.T), (2, 1)).astype(np.float32)
    C["l0_qcos"] = dup(cosf * scale).astype(bf)
    C["l0_qsin"] = dup(sinf * scale).astype(bf)
    C["l0_kcos"] = dup(cosf / scale).astype(bf)
    C["l0_ksin"] = dup(sinf / scale).astype(bf)

    inv1 = 1.0 / 10000.0 ** (np.arange(0, 64, 2, dtype=np.float64) / 64.0)
    f1 = t_ar[:, None] * inv1[None, :]
    emb = np.repeat(f1, 2, axis=-1)
    C["l1_cos"] = dup(np.cos(emb)).astype(bf)
    C["l1_sin"] = dup(np.sin(emb)).astype(bf)

    i_ar = np.arange(128)[:, None]
    j_ar = np.arange(384)[None, :]
    m = np.zeros((128, 384), dtype=np.float32)
    m[(j_ar < 128) & (j_ar < i_ar)] = -1e30
    m[(j_ar >= 256) & (j_ar - 256 > i_ar)] = -1e30
    C["mask_l0"] = m

    j_ar = np.arange(896)[None, :]
    m1 = np.zeros((128, 896), dtype=np.float32)
    m1[(j_ar >= 384) & (j_ar < 512) & (j_ar - 384 > i_ar)] = -1e30
    m1[:, 512:] = -1e30
    C["mask_l1"] = m1

    C["ones_row"] = np.ones((1, 512), dtype=np.float32)
    C["ones_row_bf"] = np.ones((1, 512), dtype=np.float32).astype(bf)
    C["ones_col"] = np.ones((128, 1), dtype=np.float32)
    bo2 = np.zeros((128, 2), dtype=np.float32)
    bo2[0:64, 0] = 1.0
    bo2[64:128, 1] = 1.0
    C["bo2"] = bo2
    bc2 = np.zeros((2, 128), dtype=np.float32)
    bc2[0, 0:64] = 1.0
    bc2[1, 64:128] = 1.0
    C["bc2"] = bc2
    C["ones1"] = np.ones((1, 128), dtype=np.float32)

    ccvals = CCOL_VALS
    C["ccols"] = np.tile(np.asarray(ccvals, dtype=np.float32), (128, 1))
    return C


CCOL_VALS = [1.0, -1.0, -0.5, 1.0 / 64, 1.0 / 1024, 1e-5, 1e-6, 0.5, -8.0, 0.125]
CCOL_IDX = {v: i for i, v in enumerate(CCOL_VALS)}



def _bcol_layout():
    L = {}
    c = 0

    def add(name, n):
        nonlocal c
        L[name] = (c, n)
        c += n
    add("ab1", 48)
    add("ab2", 8)
    add("init_h", 16)
    add("ln_g", 48)
    add("ln_b", 48)
    add("qs", 6)
    add("ks", 6)
    add("fb1", 6 * 44)
    add("fb2", 6 * 8)
    add("bs1", 8)
    add("bs3", 8)
    add("be1", 64)
    add("be3", 64)
    L["_total"] = c
    return L


BCOL = _bcol_layout()
NDERIV = 48 + 16


def _pack_vec_cols(v, nch):
    out = np.zeros((128, nch), dtype=np.float32)
    v = np.asarray(v, dtype=np.float32).ravel()
    for c in range(nch):
        seg = v[c * 128:(c + 1) * 128]
        out[:len(seg), c] = seg
    return out


def _host_biascol(inp):
    nb = BCOL["_total"]
    blob = np.zeros((128, nb), dtype=np.float32)

    def put(name, arr):
        s, n = BCOL[name]
        assert arr.shape == (128, n), (name, arr.shape)
        blob[:, s:s + n] = arr
    put("ab1", _pack_vec_cols(inp["L0_attn_b1"], 48))
    put("ab2", _pack_vec_cols(inp["L0_attn_b2"], 8))
    put("init_h", _pack_vec_cols(inp["L0_attn_init_h"], 16))
    put("ln_g", np.concatenate([_pack_vec_cols(inp["L0_ffn_ln_g"][i], 8)
                                for i in range(6)], 1))
    put("ln_b", np.concatenate([_pack_vec_cols(inp["L0_ffn_ln_b"][i], 8)
                                for i in range(6)], 1))
    put("qs", np.stack([np.tile(inp["L0_ffn_q_scale"][i], 2)
                        for i in range(6)], 1).astype(np.float32))
    put("ks", np.stack([np.tile(inp["L0_ffn_k_scale"][i], 2)
                        for i in range(6)], 1).astype(np.float32))
    fb1 = []
    for i in range(6):
        b = np.asarray(inp["L0_ffn_b1"][i])
        fb1.append(_pack_vec_cols(b[:2730], 22))
        fb1.append(_pack_vec_cols(b[2730:], 22))
    put("fb1", np.concatenate(fb1, 1))
    put("fb2", np.concatenate([_pack_vec_cols(inp["L0_ffn_b2"][i], 8)
                               for i in range(6)], 1))
    put("bs1", _pack_vec_cols(inp["L1_ffn_bs1"], 8))
    put("bs3", _pack_vec_cols(inp["L1_ffn_bs3"], 8))
    put("be1", np.concatenate([_pack_vec_cols(inp["L1_ffn_be1"][e], 4)
                               for e in range(16)], 1))
    put("be3", np.concatenate([_pack_vec_cols(inp["L1_ffn_be3"][e], 4)
                               for e in range(16)], 1))
    return blob



INPUT_SPECS = [
    ("L0_attn_W1", [1024, 6144]), ("L0_attn_W2", [2048, 1024]),
    ("L0_ffn_Wqkv", [6, 1024, 3072]), ("L0_ffn_Wo", [6, 1024, 1024]),
    ("L0_ffn_W1", [6, 1024, 5460]), ("L0_ffn_W2", [6, 2730, 1024]),
    ("L1_attn_w_attn", [1024, 1536]), ("L1_attn_w_proj", [1024, 1024]),
    ("L1_ffn_We1", [16, 1024, 512]), ("L1_ffn_We2", [16, 512, 1024]),
    ("L1_ffn_We3", [16, 1024, 512]),
    ("L1_ffn_ws1", [1024, 1024]), ("L1_ffn_ws2", [1024, 1024]),
    ("L1_ffn_ws3", [1024, 1024]),
    ("L1_ffn_be2", [16, 1024]), ("L1_ffn_bs2", [1024]),
    ("L1_ffn_gate_b", [16]),
    ("xT", [1024, 2048]), ("biascol", [128, BCOL["_total"]]),
    ("gate_wT", [1024, 16]), ("gateb_tile", [128, 16]),
]


class Ctx:

    def __init__(self, load_xres, load_xn, store_state, np_consts):
        nc = bass.Bass("TRN2", target_bir_lowering=False)
        self.nc = nc
        self.p = Prog(nc)
        p = self.p

        self.din = {}
        for name, shape in INPUT_SPECS:
            self.din[name] = nc.dram_tensor(name, list(shape), F32,
                                            kind="ExternalInput").ap()
        self.y_out = nc.dram_tensor("y", [D, T], F32, kind="ExternalOutput").ap()
        self.load_xres = load_xres
        self.load_xn = load_xn
        self.store_state = store_state
        if load_xres:
            self.xres_in = nc.dram_tensor("xres_in", [D, T], F32,
                                          kind="ExternalInput").ap()
        if load_xn:
            self.xn_in = nc.dram_tensor("xn_in", [D, T], F32,
                                        kind="ExternalInput").ap()
        if store_state:
            self.xres_st = nc.dram_tensor("xres_st", [D, T], F32,
                                          kind="ExternalOutput").ap()
            self.xn_st = nc.dram_tensor("xn_st", [D, T], F32,
                                        kind="ExternalOutput").ap()
        self.xres_spill = nc.dram_tensor("xres_spill", [D, T], F32,
                                         kind="Internal").ap()
        self.dconst = {k: nc.inline_tensor(v, name="c_" + k).ap()
                       for k, v in np_consts.items()}

        base = (nc.sbuf_base + 31) & ~31
        off = base

        def place(nbytes):
            nonlocal off
            o = off
            off += (nbytes + 31) & ~31
            return o
        self.o_xn = place(NCH * PITCH * 4)
        self.o_biascol = place((BCOL["_total"] + NDERIV) * 4)
        o_core = place(5056)
        self.o_lset = place(18944)
        self.o_xres = place(NCH * T * 4)
        arena_base = off
        arena_size = nc.sbuf_top - arena_base
        assert arena_size >= 50000, f"arena too small: {arena_size}"
        self.arena = Arena(nc, arena_base, arena_size)
        self.arena_l0ffn = Arena(nc, self.o_xres, (arena_base - self.o_xres)
                                 + arena_size)

        self.XN = nc.alloc_sbuf_tensor_at("XN", [128, NCH * PITCH], F32R,
                                          offset=self.o_xn).ap()
        self.XRES = nc.alloc_sbuf_tensor_at("XRES", [128, NCH * T], F32R,
                                            offset=self.o_xres).ap()
        self.bcol = nc.alloc_sbuf_tensor_at(
            "bcol", [128, BCOL["_total"] + NDERIV], F32, offset=self.o_biascol).ap()

        co = o_core

        def cplace(name, shape, dtype):
            nonlocal co
            h = nc.alloc_sbuf_tensor_at("k_" + name, list(shape), dtype, offset=co)
            co += (int(np.prod(shape[1:])) * mybir.dt.size(dtype) + 31) & ~31
            return h.ap()
        self.ident_f32 = cplace("idf", [128, 128], F32)
        self.ident_bf = cplace("idb", [128, 128], BF16)
        self.ones_row = cplace("onr", [1, 512], F32R)
        self.ones_row_bf = cplace("onrb", [1, 512], BF16)
        self.ones_col = cplace("onc", [128, 1], F32R)
        self.bo2 = cplace("bo2", [128, 2], F32R)
        self.bc2 = cplace("bc2", [2, 128], F32R)
        self.ones1 = cplace("on1", [1, 128], F32R)
        self.ccols = cplace("ccl", [128, len(CCOL_VALS)], F32)
        self.gih = cplace("gih", [128, 16], F32)
        assert co <= o_core + 5056, co

        lo = self.o_lset

        def lplace(name, shape, dtype):
            nonlocal lo
            h = nc.alloc_sbuf_tensor_at("k_" + name, list(shape), dtype, offset=lo)
            lo += (int(np.prod(shape[1:])) * mybir.dt.size(dtype) + 31) & ~31
            return h.ap()
        self.l0_qcos = lplace("qc", [128, T], BF16)
        self.l0_qsin = lplace("qs_", [128, T], BF16)
        self.l0_kcos = lplace("kc", [128, T], BF16)
        self.l0_ksin = lplace("ks_", [128, T], BF16)
        self.mask_l0 = lplace("ml0", [128, 384], F32)
        self.rl0 = lplace("rl0", [128, 128], F32R)
        assert lo <= self.o_lset + 18944, lo
        lo = self.o_lset
        self.l1_cos = lplace("l1c", [128, T], BF16)
        self.l1_sin = lplace("l1s", [128, T], BF16)
        self.mask_l1 = lplace("ml1", [128, 896], F32)
        self.rl1 = lplace("rl1", [128, 128], F32R)
        self.gateb = lplace("gtb", [128, 16], F32)
        self.be2sb = lplace("be2", [16, 1024], BF16)
        self.bs2row = lplace("bs2", [1, 1024], F32R)
        assert lo <= self.o_lset + 18944, lo

        self.bank = [nc.alloc_psum_tensor(f"bank{i}", [128, 512], F32).ap()
                     for i in range(8)]

    def bc(self, name, i=0):
        s, n = BCOL[name]
        assert i < n, (name, i)
        return self.bcol[:, s + i:s + i + 1]

    def bcr(self, name, i, n):
        s, _ = BCOL[name]
        return self.bcol[:, s + i:s + i + n]

    def negb(self, i):
        return self.bcol[:, BCOL["_total"] + i:BCOL["_total"] + i + 1]

    def cc(self, val):
        return self.ccols[:, CCOL_IDX[val]:CCOL_IDX[val] + 1]

    def xn(self, k, t0=0, n=T, shift=False):
        b = k * PITCH + GOFF + t0 - (1 if shift else 0)
        return self.XN[:, b:b + n]

    def xres(self, k, t0=0, n=T):
        b = k * T + t0
        return self.XRES[:, b:b + n]


def emit_init(ctx, first_launch):
    p, nc = ctx.p, ctx.nc
    for name, ap in (("ident_f32", ctx.ident_f32), ("ident_bf", ctx.ident_bf),
                     ("mask_l0", ctx.mask_l0), ("ccols", ctx.ccols)):
        p.dma("DQ", ap, ctx.dconst[name], ch="consts")
    for name, ap in (("ones_row", ctx.ones_row), ("ones_col", ctx.ones_col),
                     ("bo2", ctx.bo2), ("bc2", ctx.bc2), ("ones1", ctx.ones1),
                     ("rl0", ctx.rl0)):
        p.dma("DG", ap, ctx.dconst[name], ch="gconsts")
    p.dma("DQ", ctx.ones_row_bf, ctx.dconst["ones_row_bf"], ch="consts")
    for name, ap in (("l0_qcos", ctx.l0_qcos), ("l0_qsin", ctx.l0_qsin),
                     ("l0_kcos", ctx.l0_kcos), ("l0_ksin", ctx.l0_ksin)):
        p.dma("DQ", ap, ctx.dconst[name], ch="consts")
    p.dma("DQ", ctx.bcol[:, 0:BCOL["_total"]], ctx.din["biascol"], ch="consts")

    if first_launch:
        for k in range(NCH):
            p.dma("DG", ctx.xres(k), ctx.din["xT"][k * 128:(k + 1) * 128, :])
    elif ctx.load_xres:
        for k in range(NCH):
            p.dma("DG", ctx.xres(k), ctx.xres_in[k * 128:(k + 1) * 128, :])
    if ctx.load_xn:
        for k in range(NCH):
            p.dma("DG", ctx.xn(k), ctx.xn_in[k * 128:(k + 1) * 128, :])

    p.barrier()
    for k in range(NCH):
        b = k * PITCH + GOFF - 1
        p.memset(ctx.XN[:, b:b + 1].bitcast(F32), 0.0)

    s, n = BCOL["ab1"]
    p.ts(ctx.bcol[:, BCOL["_total"]:BCOL["_total"] + 48],
         ctx.bcol[:, s:s + 48], -1.0, ALU.mult)
    ih = ctx.bcr("init_h", 0, 16)
    a = ctx.arena
    a.reset()
    t1 = a.alloc([128, 16], F32, "init1")
    t2 = a.alloc([128, 16], F32, "init2")
    p.act(t1, ih, AF.Exp, scale=ctx.cc(-1.0))
    if DEBUG_L0A:
        p.dma("DQ", ctx.y_out[512:640, 0:16], t1)
    p.ts(t1, t1, 1.0, ALU.add)
    if DEBUG_L0A:
        p.dma("DQ", ctx.y_out[512:640, 16:32], t1)
    p.recip(t2, t1)
    if DEBUG_L0A:
        p.dma("DQ", ctx.y_out[512:640, 32:48], t2)
    p.ts(t2, t2, 0.5, ALU.min)
    if DEBUG_L0A:
        p.dma("DQ", ctx.y_out[512:640, 48:64], t2)
    p.act(t1, ih, AF.Relu)
    if DEBUG_L0A:
        p.dma("DQ", ctx.y_out[512:640, 64:80], t1)
    p.tt(ctx.gih, t1, t2, ALU.add)
    p.barrier()


def emit_store_state(ctx):
    p = ctx.p
    for k in range(NCH):
        p.dma("DQ", ctx.xres_st[k * 128:(k + 1) * 128, :],
              ctx.xres(k).bitcast(F32))
        p.dma("DQ", ctx.xn_st[k * 128:(k + 1) * 128, :],
              ctx.xn(k).bitcast(F32))


def emit_output(ctx):
    p = ctx.p
    for k in range(NCH):
        p.dma("DQ", ctx.y_out[k * 128:(k + 1) * 128, :],
              ctx.xres(k).bitcast(F32))


def emit_finish(ctx):
    p = ctx.p
    for sk, v in p.cnt.items():
        if sk.startswith("D_") and v > 0:
            p.eng("Q").wait_ge(p.sem[sk], v)


def emit_ln(ctx, dst_is_xn=True, affine_layer=None, src_shift=False):
    p = ctx.p
    a = ctx.arena
    a.reset()
    sq = a.alloc([128, T], F32R, "lnsq")
    murow = a.alloc([1, T], F32R, "lnmu")
    rsrow = a.alloc([1, T], F32R, "lnrs")
    trow = a.alloc([1, 512], F32, "lnt")
    tn = a.alloc([128, 512], F32, "lnn")
    for s in range(4):
        t0 = s * 512
        ps_sum = ctx.bank[0][0:1, :]
        ps_ssq = ctx.bank[1][0:1, :]
        for k in range(NCH):
            p.act(sq[:, t0:t0 + 512], ctx.xres(k, t0, 512), AF.Square)
            p.mm(ps_sum, ctx.ones_col, ctx.xres(k, t0, 512),
                 start=(k == 0), stop=(k == NCH - 1))
            p.mm(ps_ssq, ctx.ones_col, sq[:, t0:t0 + 512].bitcast(F32R),
                 start=(k == 0), stop=(k == NCH - 1))
        mu = murow[0:1, t0:t0 + 512]
        rs = rsrow[0:1, t0:t0 + 512]
        p.ts(mu, ps_sum, 1.0 / D, ALU.mult)
        p.tt(trow, mu, mu, ALU.mult)
        p.ts(trow, trow, -1.0, ALU.mult)
        p.stt(trow, ps_ssq, 1.0 / D, trow, ALU.mult, ALU.add)
        p.act(trow, trow, AF.Ln, bias=ctx.cc(1e-5))
        p.act(rs, trow, AF.Exp, scale=ctx.cc(-0.5))
        ps_bmu = ctx.bank[2]
        ps_brs = ctx.bank[3]
        p.mm(ps_bmu, ctx.ones1, mu, start=True, stop=True)
        p.mm(ps_brs, ctx.ones1, rs, start=True, stop=True)
        for k in range(NCH):
            p.tt(tn, ctx.xres(k, t0, 512), ps_bmu, ALU.subtract)
            p.tt(ctx.xn(k, t0, 512), tn, ps_brs, ALU.mult)
    p.barrier()



def _phase(ctx, name):
    if name == "ln0":
        emit_ln(ctx)
    elif name == "addstream":
        for k in range(NCH):
            ctx.p.tt(ctx.xres(k), ctx.xres(k), ctx.xn(k), ALU.add)
        ctx.p.barrier()
    elif name == "l0attn":
        emit_l0_attn(ctx)
    elif name.startswith("l0ffn"):
        emit_l0_ffn_layer(ctx, int(name[5:]))
    elif name == "ln2":
        emit_ln(ctx)
        emit_l1_consts(ctx)
    elif name == "moe":
        emit_moe(ctx)
    elif name == "l1attn":
        emit_l1_attn(ctx)
    else:
        raise ValueError(name)


ALL_PHASES = ["ln0", "l0attn"] + [f"l0ffn{i}" for i in range(6)] + \
    ["addstream", "ln2", "moe", "l1attn"]

LAUNCH_PLAN = [ALL_PHASES]

_SPILL_PHASES = {f"l0ffn{i}" for i in range(6)}


def build_launch(phases, first, last, np_consts):
    needs_xres = any(ph not in _SPILL_PHASES for ph in phases)
    needs_xn = phases[0] != "ln0"
    ctx = Ctx(load_xres=(not first) and needs_xres,
              load_xn=(not first) and needs_xn,
              store_state=not last, np_consts=np_consts)
    emit_init(ctx, first_launch=first)
    prev_spill = False
    for ph in phases:
        spill = ph in _SPILL_PHASES
        if spill and not prev_spill and needs_xres and len(phases) > 1:
            for k in range(NCH):
                ctx.p.dma("DQ", ctx.xres_spill[k * 128:(k + 1) * 128, :],
                          ctx.xres(k).bitcast(F32))
            ctx.p.barrier()
        if not spill and prev_spill and needs_xres and len(phases) > 1:
            for k in range(NCH):
                ctx.p.dma("DG", ctx.xres(k),
                          ctx.xres_spill[k * 128:(k + 1) * 128, :])
            ctx.p.barrier()
        prev_spill = spill
        _phase(ctx, ph)
    if not last:
        emit_store_state(ctx)
    else:
        emit_output(ctx)
    emit_finish(ctx)
    return ctx


def _host_inputs(inputs, core):
    m = {}
    for name, shape in INPUT_SPECS:
        if name == "xT":
            m[name] = np.ascontiguousarray(inputs["L0_attn_x"][core].T)
        elif name == "biascol":
            m[name] = _host_biascol(inputs)
        elif name == "gate_wT":
            m[name] = np.ascontiguousarray(inputs["L1_ffn_gate_w"].T)
        elif name == "gateb_tile":
            m[name] = np.ascontiguousarray(
                np.tile(np.asarray(inputs["L1_ffn_gate_b"])[None, :], (128, 1)))
        else:
            m[name] = np.asarray(inputs[name])
    return m


def kernel(**inputs):
    np_consts = _np_consts()
    base_maps = [_host_inputs(inputs, core) for core in range(8)]
    state = [dict() for _ in range(8)]
    out = np.empty((8, T, D), dtype=np.float32)
    nl = len(LAUNCH_PLAN)
    for li, phases in enumerate(LAUNCH_PLAN):
        first, last = li == 0, li == nl - 1
        ctx = build_launch(phases, first, last, np_consts)
        in_maps = []
        for core in range(8):
            m = dict(base_maps[core])
            if not first:
                if ctx.load_xres:
                    m["xres_in"] = state[core]["xres"]
                if ctx.load_xn:
                    m["xn_in"] = state[core]["xn"]
            in_maps.append(m)
        res = run_bass_kernel_spmd(ctx.nc, in_maps, core_ids=list(range(8)))
        for core in range(8):
            r = res.results[core]
            if not last:
                if "xres_st" in r:
                    state[core]["xres"] = r["xres_st"]
                state[core]["xn"] = r["xn_st"]
            else:
                out[core] = r["y"].T
        if not last:
            for core in range(8):
                if "xres_st" not in res.results[core] or not ctx.store_state:
                    pass
    return out



def emit_l0_attn(ctx):
    p = ctx.p
    a = ctx.arena
    a.reset()
    W1 = ctx.din["L0_attn_W1"]
    W2 = ctx.din["L0_attn_W2"]
    wring = [a.alloc([128, 1024], F32R, f"aw1_{i}") for i in range(3)]
    w2ring = [a.alloc([128, 1024], BF16, f"aw2_{i}") for i in range(2)]
    t_a = a.alloc([128, 512], F32, "ga")
    t_b = a.alloc([128, 512], F32, "gb")
    t_uw = a.alloc([128, 1024], F32, "guw")
    t_r = a.alloc([128, 1024], F32, "gr")
    CVc = a.alloc([128, T], F32, "cvc")
    CVv = a.alloc([128, T], F32, "cvv")
    OH = [a.alloc([128, T], BF16, f"oh{i}") for i in range(2)]

    for c in range(16):
        for g in range(3):
            src = W1.rearrange("(kc kp) m -> kp kc m", kp=128)[
                :, :, g * 2048 + c * 128: g * 2048 + (c + 1) * 128]
            p.dma("DG", wring[g], src)
        for s in range(4):
            t0 = s * 512
            psb = 0 if s % 2 == 0 else 3
            ps_h, ps_f, ps_i = (ctx.bank[psb], ctx.bank[psb + 1],
                                ctx.bank[psb + 2])
            for g, ps in ((0, ps_h), (1, ps_f), (2, ps_i)):
                for k in range(NCH):
                    p.mm(ps, wring[g][:, k * 128:(k + 1) * 128],
                         ctx.xn(k, t0, 512), start=(k == 0), stop=(k == NCH - 1))
            cslice = CVc[:, t0:t0 + 512]
            vslice = CVv[:, t0:t0 + 512]
            u = t_uw[:, 0:512]
            w = t_uw[:, 512:1024]
            ru = t_r[:, 0:512]
            rw = t_r[:, 512:1024]
            p.act(t_a, ps_f, AF.Exp, scale=ctx.cc(-1.0), bias=ctx.negb(16 + c))
            p.act(t_b, ps_i, AF.Exp, scale=ctx.cc(-1.0), bias=ctx.negb(32 + c))
            p.tt(u, t_a, t_b, ALU.add)
            p.ts(u, u, 2.0, ALU.add)
            p.ts(w, ps_h, ctx.bc("ab1", c), ALU.add, 0.0, ALU.min)
            p.act(w, w, AF.Exp)
            p.ts(w, w, 1.0, ALU.add)
            p.recip(t_r, t_uw)
            p.ts(t_b, t_b, 1.0, ALU.add)
            p.tt(cslice, t_b, ru, ALU.mult)
            p.ts(t_a, t_a, 1.0, ALU.add)
            p.tt(t_b, t_a, ru, ALU.mult)
            p.ts(w, w, -1.0, ALU.add)
            p.tt(w, w, rw, ALU.mult)
            p.act(t_a, ps_h, AF.Relu, bias=ctx.bc("ab1", c))
            p.tt(w, t_a, w, ALU.add)
            p.tt(vslice, t_b, w, ALU.mult)
        oh = OH[c % 2]
        p.scan(oh, CVc, CVv, ctx.gih[:, c:c + 1], eng="V")
        if c == 0 and DEBUG_L0A:
            p.dma("DQ", ctx.y_out[0:128, 0:1024], oh.bitcast(F32)[:, 0:1024])
            p.dma("DQ", ctx.y_out[128:256, :], CVc)
            p.dma("DQ", ctx.y_out[256:384, :], CVv)
            p.dma("DQ", ctx.y_out[384:512, 0:16], ctx.gih)
        if c % 2 == 1:
            ee = c // 2
            for j in range(2):
                cc = c - 1 + j
                p.dma("DG", w2ring[j],
                      W2[cc * 128:(cc + 1) * 128, :])
            for m in range(NCH):
                for s in range(4):
                    t0 = s * 512
                    psw = ctx.bank[6 + (m * 4 + s) % 2]
                    for j in range(2):
                        p.mm(psw, w2ring[j][:, m * 128:(m + 1) * 128],
                             OH[j][:, t0:t0 + 512], start=(j == 0), stop=(j == 1))
                    if ee < 7:
                        p.tt(ctx.xres(m, t0, 512), ctx.xres(m, t0, 512), psw,
                             ALU.add)
                    else:
                        p.stt(ctx.xres(m, t0, 512), psw, ctx.bc("ab2", m),
                              ctx.xres(m, t0, 512), ALU.add, ALU.add)
    p.barrier()



def emit_l0_ffn_layer(ctx, li):
    p = ctx.p
    a = ctx.arena_l0ffn

    Wqkv = ctx.din["L0_ffn_Wqkv"][li]
    Wo = ctx.din["L0_ffn_Wo"][li]
    W1 = ctx.din["L0_ffn_W1"][li]
    W2 = ctx.din["L0_ffn_W2"][li]

    a.reset()
    skip_attn = bool(FFN_ONLY)
    wring = [a.alloc([128, 1024], F32R, f"fw{i}") for i in range(2)]
    QBUF = [a.alloc([128, 1024], BF16, f"fq{i}") for i in range(8)]
    KBUF = [a.alloc([128, 1280], BF16, f"fk{i}") for i in range(8)]
    VBUF = [a.alloc([128, 1024], BF16, f"fv{i}") for i in range(10)]
    OBUF = [a.alloc([128, 1024], BF16, f"fo{i}") for i in range(8)]
    t_sq = a.alloc([128, 512], F32R, "fsq")
    t_qe = a.alloc([128, 512], F32R, "fqe")
    t_qn = a.alloc([128, 512], F32R, "fqn")
    t_t2 = a.alloc([128, 512], F32, "ft2")
    row_mx = a.alloc([2, 512], F32, "frmx")
    row_rs = a.alloc([2, 512], F32R, "frrs")
    t_sc = [a.alloc([128, 384], F32, f"fsc{i}") for i in range(2)]
    PBUF = [a.alloc([128, 384], BF16, f"fp{i}") for i in range(16)]
    sums = a.alloc([128, 16], F32, "fsm")
    rsums = a.alloc([128, 16], F32, "frs")
    negmx = a.alloc([128, 2], F32, "fnm")
    t_pn = [a.alloc([128, 384], BF16, f"fpn{i}") for i in range(2)]
    t_ptb = [a.alloc([128, 128], BF16, f"fpt{i}") for i in range(6)]
    t_vb = a.alloc([128, 512], BF16, "fvb")
    woring = [a.alloc([128, 1024], BF16, f"fwo{i}") for i in range(2)]

    for half in range(2 if not skip_attn else 0):
        h0 = half * 1024
        if half == 1:
            for mc in range(8):
                p.cp(KBUF[mc][:, 0:256], KBUF[mc][:, 1024:1280], eng="P")
            for j in range(2):
                p.cp(VBUF[j], VBUF[8 + j], eng="P")
        def evac_qk(mc, s, ps):
            is_q = mc < 8
            hc = mc % 8
            t0 = h0 + s * 512
            scol = ctx.bc("qs" if is_q else "ks", li)
            cost = ctx.l0_qcos if is_q else ctx.l0_kcos
            sint = ctx.l0_qsin if is_q else ctx.l0_ksin
            p.act(t_sq, ps, AF.Square)
            ps_ss = ctx.bank[4][0:2, 0:512]
            p.mm(ps_ss, ctx.bo2, t_sq, start=True, stop=True)
            p.ts(row_mx, ps_ss, 1e-24, ALU.max)
            p.act(row_mx, row_mx, AF.Ln)
            p.act(row_rs, row_mx, AF.Exp, scale=ctx.cc(-0.5))
            ps_bc = ctx.bank[5]
            p.mm(ps_bc, ctx.bc2, row_rs, start=True, stop=True)
            p.act(t_qe, ps, AF.Identity, scale=scol)
            p.tt(t_qn, t_qe, ps_bc, ALU.mult)
            ps_rot = ctx.bank[6]
            p.mm(ps_rot, ctx.rl0, t_qn, start=True, stop=True)
            dst = (QBUF[hc][:, s * 512:(s + 1) * 512] if is_q else
                   KBUF[hc][:, 256 + s * 512: 256 + (s + 1) * 512])
            p.tt(t_qe, t_qn, cost[:, t0:t0 + 512], ALU.mult)
            p.tt(t_t2, ps_rot, sint[:, t0:t0 + 512], ALU.mult)
            p.tt(dst, t_qe.bitcast(F32), t_t2, ALU.add)

        def evac_v(mc, s, ps):
            vc = mc - 16
            p.act(t_vb, ps, AF.Identity)
            for j in range(4):
                pst = ctx.bank[7].bitcast(BF16)[:, 0:128]
                p.transp(pst, t_vb[:, j * 128:(j + 1) * 128], ctx.ident_bf)
                p.cp(VBUF[2 + s * 4 + j][:, vc * 128:(vc + 1) * 128], pst,
                     eng="S")

        pend = None
        for mc in range(24):
            slab = wring[mc % 2]
            src = Wqkv.rearrange("(kc kp) m -> kp kc m", kp=128)[
                :, :, mc * 128:(mc + 1) * 128]
            p.dma("DG", slab, src)
            for s in range(2):
                t0 = h0 + s * 512
                ps = ctx.bank[(mc * 2 + s) % 4]
                for k in range(NCH):
                    p.mm(ps, slab[:, k * 128:(k + 1) * 128],
                         ctx.xn(k, t0, 512, shift=(k >= 4)),
                         start=(k == 0), stop=(k == NCH - 1))
                if pend is not None:
                    pmc, pst_, pps = pend
                    (evac_qk if pmc < 16 else evac_v)(pmc, pst_, pps)
                pend = (mc, s, ps)
        if pend is not None:
            pmc, pst_, pps = pend
            (evac_qk if pmc < 16 else evac_v)(pmc, pst_, pps)
        for w in range(4 * half, 4 * half + 4):
            for qt2 in range(2):
                qg0 = w * 256 + qt2 * 128
                kbase = max(0, qg0 - 256)
                width = 384 - (kbase - (qg0 - 256))
                kcol = kbase - (h0 - 256)
                mask = ctx.mask_l0[:, 384 - width:384]
                qcol = w * 256 + qt2 * 128 - h0
                for h in range(16):
                    hc, pb = h // 2, (h % 2) * 64
                    ps_sc = ctx.bank[(h % 2)][:, 0:width]
                    p.mm(ps_sc, QBUF[hc][pb:pb + 64, qcol:qcol + 128],
                         KBUF[hc][pb:pb + 64, kcol:kcol + width],
                         start=True, stop=True)
                    sc = t_sc[h % 2][:, 0:width]
                    p.stt(sc, ps_sc, 8.0, mask, ALU.mult, ALU.add)
                    p.red(negmx[:, h % 2:h % 2 + 1], sc, ALU.max, negate=True)
                    p.act(PBUF[h][:, 0:width], sc, AF.Exp,
                          bias=negmx[:, h % 2:h % 2 + 1],
                          accum=sums[:, h:h + 1])
                p.act(rsums, sums, AF.Ln)
                p.act(rsums, rsums, AF.Exp, scale=ctx.cc(-1.0))
                nkc = width // 128
                for h in range(16):
                    hc, pb = h // 2, (h % 2) * 64
                    pn = t_pn[h % 2][:, 0:width]
                    p.ts(pn, PBUF[h][:, 0:width], rsums[:, h:h + 1], ALU.mult)
                    ptb = ctx.bank[2 + h % 2].bitcast(BF16)
                    for kc in range(nkc):
                        p.transp(ptb[:, kc * 128:(kc + 1) * 128],
                                 pn[:, kc * 128:(kc + 1) * 128], ctx.ident_bf)
                    for kc in range(nkc):
                        p.cp(t_ptb[3 * (h % 2) + kc],
                             ptb[:, kc * 128:(kc + 1) * 128])
                    ps_o = ctx.bank[4 + h % 2][0:64, 0:128]
                    vi0 = kcol // 128
                    for kc in range(nkc):
                        p.mm(ps_o, VBUF[vi0 + kc][:, h * 64:(h + 1) * 64],
                             t_ptb[3 * (h % 2) + kc],
                             start=(kc == 0), stop=(kc == nkc - 1))
                    p.act(OBUF[hc][pb:pb + 64, qcol:qcol + 128], ps_o, AF.Identity)
        for s in range(2):
            for oc in range(8):
                slab = woring[oc % 2]
                p.dma("DG", slab, Wo[oc * 128:(oc + 1) * 128, :])
                for m in range(8):
                    psw = ctx.bank[m]
                    p.mm(psw, slab[:, m * 128:(m + 1) * 128],
                         OBUF[oc][:, s * 512:(s + 1) * 512],
                         start=(oc == 0), stop=(oc == 7))
                    if oc == 7:
                        t0 = h0 + s * 512
                        p.tt(ctx.xn(m, t0, 512), ctx.xn(m, t0, 512), psw,
                             ALU.add)
    p.barrier()

    if FFN_ONLY is not None and not FFN_ONLY:
        return
    a.reset()
    XN2 = [a.alloc([128, T], BF16, f"x2{i}") for i in range(8)]
    H = [a.alloc([128, 1024], BF16, f"hh{i}") for i in range(22)]
    mark0 = a.mark()

    sq_sl = a.alloc([128, 512], F32R, "fsq2")
    t_n = a.alloc([128, 512], F32, "ftn")
    row_mu = a.alloc([1, 512], F32R, "fmu")
    row_rs2 = a.alloc([1, 512], F32R, "frs2")
    row_t = a.alloc([1, 512], F32, "frt")
    for s in range(4):
        t0 = s * 512
        ps_sum = ctx.bank[0][0:1, 0:512]
        ps_ssq = ctx.bank[1][0:1, 0:512]
        for k in range(NCH):
            src = ctx.xn(k, t0, 512, shift=(k >= 4))
            p.act(sq_sl, src, AF.Square)
            p.mm(ps_sum, ctx.ones_col, src, start=(k == 0), stop=(k == NCH - 1))
            p.mm(ps_ssq, ctx.ones_col, sq_sl, start=(k == 0), stop=(k == NCH - 1))
        p.ts(row_mu, ps_sum, 1.0 / D, ALU.mult)
        p.tt(row_t, row_mu, row_mu, ALU.mult)
        p.ts(row_t, row_t, -1.0, ALU.mult)
        p.stt(row_t, ps_ssq, 1.0 / D, row_t, ALU.mult, ALU.add)
        p.act(row_t, row_t, AF.Ln, bias=ctx.cc(1e-5))
        p.act(row_rs2, row_t, AF.Exp, scale=ctx.cc(-0.5))
        ps_bmu = ctx.bank[2]
        ps_brs = ctx.bank[3]
        p.mm(ps_bmu, ctx.ones1, row_mu, start=True, stop=True)
        p.mm(ps_brs, ctx.ones1, row_rs2, start=True, stop=True)
        for k in range(NCH):
            src = ctx.xn(k, t0, 512, shift=(k >= 4))
            p.tt(t_n, src, ps_bmu, ALU.subtract)
            p.tt(t_n, t_n, ps_brs, ALU.mult)
            p.act(XN2[k][:, t0:t0 + 512], t_n, AF.Identity,
                  scale=ctx.bc("ln_g", li * 8 + k), bias=ctx.bc("ln_b", li * 8 + k))
    p.barrier()

    for half in range(2):
        h0 = half * 1024
        a.reset_to(mark0)
        wa_ring = [a.alloc([128, 1024], BF16, f"fwa{i}") for i in range(2)]
        wg_ring = [a.alloc([128, 1024], BF16, f"fwg{i}") for i in range(2)]
        t_g = a.alloc([128, 512], F32, f"ftg")
        for j in range(22):
            aw = 128 if j < 21 else 42
            wa = wa_ring[j % 2]
            wg = wg_ring[j % 2]
            p.dma("DG", wa[:, 0:8 * aw],
                  W1.rearrange("(kc kp) m -> kp kc m", kp=128)[
                      :, :, j * 128:j * 128 + aw])
            p.dma("DG", wg[:, 0:8 * aw],
                  W1.rearrange("(kc kp) m -> kp kc m", kp=128)[
                      :, :, 2730 + j * 128:2730 + j * 128 + aw])
            for s in range(2):
                t0 = h0 + s * 512
                psA = ctx.bank[0 + (s % 2) * 2][0:aw, :]
                psB = ctx.bank[1 + (s % 2) * 2][0:aw, :]
                for k in range(NCH):
                    p.mm(psA, wa[:, k * aw:(k + 1) * aw], XN2[k][:, t0:t0 + 512],
                         start=(k == 0), stop=(k == NCH - 1))
                for k in range(NCH):
                    p.mm(psB, wg[:, k * aw:(k + 1) * aw], XN2[k][:, t0:t0 + 512],
                         start=(k == 0), stop=(k == NCH - 1))
                p.act(t_g[0:aw, :], psB, AF.Gelu,
                      bias=ctx.bc("fb1", li * 44 + 22 + j))
                p.stt(H[j][0:aw, s * 512:(s + 1) * 512], psA,
                      ctx.bc("fb1", li * 44 + j), t_g[0:aw, :],
                      ALU.add, ALU.mult)
        p.barrier()
        a.reset_to(mark0)
        w2all = [a.alloc([128, 512], BF16, f"f2w{i}") for i in range(22)]
        for mh in range(2):
            for ic in range(22):
                ah = 128 if ic < 21 else 42
                p.dma("DG", w2all[ic][0:ah, :],
                      W2[ic * 128:ic * 128 + ah, mh * 512:(mh + 1) * 512])
            for mi in range(4):
                m = mh * 4 + mi
                for s in range(2):
                    t0 = h0 + s * 512
                    psw = ctx.bank[4 + (mi * 2 + s) % 4]
                    for ic in range(22):
                        ah = 128 if ic < 21 else 42
                        p.mm(psw, w2all[ic][0:ah, mi * 128:(mi + 1) * 128],
                             H[ic][0:ah, s * 512:(s + 1) * 512],
                             start=(ic == 0), stop=(ic == 21))
                    p.stt(ctx.xn(m, t0, 512), psw, ctx.bc("fb2", li * 8 + m),
                          ctx.xn(m, t0, 512), ALU.add, ALU.add)
        p.barrier()



def emit_l1_consts(ctx):
    p = ctx.p
    for name, ap in (("l1_cos", ctx.l1_cos), ("l1_sin", ctx.l1_sin),
                     ("mask_l1", ctx.mask_l1)):
        p.dma("DQ", ap, ctx.dconst[name], ch="consts")
    p.dma("DG", ctx.rl1, ctx.dconst["rl1"], ch="gconsts")
    p.dma("DQ", ctx.gateb, ctx.din["gateb_tile"], ch="consts")
    p.barrier()


def emit_moe(ctx):
    p = ctx.p
    a = ctx.arena
    a.reset()
    COMB = a.alloc([128, 256], F32, "mcb")
    CF = a.alloc([16, T], BF16, "mcf")
    gslab = a.alloc([128, 128], F32R, "mgw")
    gmark = a.mark()
    LG = a.alloc([128, 256], F32, "mlg")
    EX = a.alloc([128, 256], F32, "mex")
    ORIG = a.alloc([128, 256], F32, "mor")
    SC = a.alloc([128, 256], F32, "msc")
    CNT = a.alloc([128, 256], F32, "mcn")
    TMP = a.alloc([128, 256], F32, "mtp")
    MX = a.alloc([128, 16], F32, "mmx")
    SM = a.alloc([128, 16], F32, "msm")
    RE = a.alloc([128, 16], F32, "mre")
    GS = a.alloc([128, 64], F32, "mgs")
    GT = a.alloc([128, 64], F32, "mgt")
    KEEP = a.alloc([128, 64], F32, "mkp")
    be2sb = ctx.be2sb
    bs2row = ctx.bs2row

    gwT = ctx.din["gate_wT"]
    p.dma("DG", gslab, gwT.rearrange("(kc kp) e -> kp kc e", kp=128))
    p.dma("DG", be2sb, ctx.din["L1_ffn_be2"])
    p.dma("DG", bs2row, ctx.din["L1_ffn_bs2"].unsqueeze(0))

    for tc in range(16):
        ps_g = ctx.bank[tc % 2][:, 0:16]
        for k in range(NCH):
            p.mm(ps_g, ctx.xn(k, tc * 128, 128), gslab[:, k * 16:(k + 1) * 16],
                 start=(k == 0), stop=(k == NCH - 1))
        p.cp(LG[:, tc * 16:(tc + 1) * 16], ps_g)
    lg3 = LG.rearrange("p (a b) -> p a b", b=16)
    ex3 = EX.rearrange("p (a b) -> p a b", b=16)
    or3 = ORIG.rearrange("p (a b) -> p a b", b=16)
    sc3 = SC.rearrange("p (a b) -> p a b", b=16)
    p.red(MX, lg3, ALU.max)
    mx3 = MX.rearrange("p (a b) -> p a b", b=1).broadcast_to([128, 16, 16])
    p.tt(ex3, lg3, mx3, ALU.subtract)
    p.act(EX, EX, AF.Exp)
    p.red(SM, ex3, ALU.add)
    p.act(SM, SM, AF.Ln)
    p.act(RE, SM, AF.Exp, scale=ctx.cc(-1.0))
    re3 = RE.rearrange("p (a b) -> p a b", b=1).broadcast_to([128, 16, 16])
    p.tt(or3, ex3, re3, ALU.mult)
    gb3 = ctx.gateb.rearrange("p (a b) -> p a b", a=1).broadcast_to([128, 16, 16])
    p.tt(sc3, or3, gb3, ALU.add)
    sc4 = SC.rearrange("p (a g e) -> p (a g) e", g=4, e=4)
    gs_first = True
    for i in range(4):
        for j in range(i + 1, 4):
            dst = GS if gs_first else GT
            p.tt(dst, sc4[:, :, i:i + 1].squeeze(2), sc4[:, :, j:j + 1].squeeze(2),
                 ALU.add)
            if not gs_first:
                p.tt(GS, GS, GT, ALU.max)
            gs_first = False
    gs3 = GS.rearrange("p (a g) -> p a g", g=4)
    p.memset(KEEP, 0.0)
    for gp in range(4):
        col = gs3[:, :, gp:gp + 1].broadcast_to([128, 16, 4])
        k3 = KEEP.rearrange("p (a g) -> p a g", g=4)
        p.tt(TMP[:, 0:64].rearrange("p (a g) -> p a g", g=4), gs3, col, ALU.is_lt)
        p.tt(KEEP, KEEP, TMP[:, 0:64], ALU.add)
        if gp < 3:
            sl = gs3[:, :, gp + 1:4]
            colp = gs3[:, :, gp:gp + 1].broadcast_to([128, 16, 3 - gp])
            p.tt(TMP[:, 0:64].rearrange("p (a g) -> p a g", g=4)[:, :, gp + 1:4],
                 sl, colp, ALU.is_equal)
            p.tt(k3[:, :, gp + 1:4], k3[:, :, gp + 1:4],
                 TMP[:, 0:64].rearrange("p (a g) -> p a g", g=4)[:, :, gp + 1:4],
                 ALU.add)
    p.ts(KEEP, KEEP, 2.0, ALU.is_lt)
    p.cp(TMP.rearrange("p (x e) -> p x e", e=4),
         KEEP.unsqueeze(2).broadcast_to([128, 64, 4]))
    T2 = a.alloc([128, 256], F32, "mt2a")
    p.tt(T2, SC, TMP, ALU.mult)
    p.ts(SC, TMP, 1e30, ALU.mult, -1e30, ALU.add)
    p.tt(SC, T2, SC, ALU.add)
    p.memset(CNT, 0.0)
    m3 = SC.rearrange("p (a b) -> p a b", b=16)
    c3 = CNT.rearrange("p (a b) -> p a b", b=16)
    t23 = T2.rearrange("p (a b) -> p a b", b=16)
    for ep in range(16):
        col = m3[:, :, ep:ep + 1].broadcast_to([128, 16, 16])
        p.tt(t23, m3, col, ALU.is_lt)
        p.tt(CNT, CNT, T2, ALU.add)
        if ep < 15:
            sl = m3[:, :, ep + 1:16]
            colp = m3[:, :, ep:ep + 1].broadcast_to([128, 16, 15 - ep])
            p.tt(t23[:, :, ep + 1:16], sl, colp, ALU.is_equal)
            p.tt(c3[:, :, ep + 1:16], c3[:, :, ep + 1:16], t23[:, :, ep + 1:16],
                 ALU.add)
    p.ts(CNT, CNT, 2.0, ALU.is_lt)
    p.tt(COMB, ORIG, CNT, ALU.mult)
    if DEBUG_MOE:
        p.dma("DQ", ctx.y_out[0:128, 0:256], COMB)
        p.dma("DQ", ctx.y_out[128:256, 0:256], ORIG)
        p.dma("DQ", ctx.y_out[256:384, 0:256], SC)
        p.dma("DQ", ctx.y_out[384:512, 0:64], GS)
        p.dma("DQ", ctx.y_out[512:640, 0:64], KEEP)
        p.dma("DQ", ctx.y_out[640:768, 0:256], CNT)
        p.dma("DQ", ctx.y_out[768:896, 0:256], TMP)
    for tc in range(16):
        ps_t = ctx.bank[2][0:16, 0:128]
        cslice = COMB[:, tc * 16:(tc + 1) * 16]
        p.transp(ps_t, cslice.bitcast(F32), ctx.ident_f32)
        p.cp(CF[0:16, tc * 128:(tc + 1) * 128], ps_t)
    p.barrier()
    a.reset_to(gmark)
    y_acc = [a.alloc([128, 1024], F32, f"my{i}") for i in range(4)]
    H = [a.alloc([128, 512], BF16, f"mh{i}") for i in range(4)]
    HS = [a.alloc([128, 512], BF16, f"ms{i}") for i in range(8)]
    w1r = [a.alloc([128, 1024], F32R, f"mw1{i}") for i in range(2)]
    w3r = [a.alloc([128, 1024], F32R, f"mw3{i}") for i in range(2)]
    w2r = [a.alloc([128, 1024], BF16, f"mw2{i}") for i in range(1)]
    t_si = a.alloc([128, 512], F32, "msi")

    for s in range(4):
        t0 = s * 512
        for tt in range(4):
            for dh in range(2):
                ps_y = ctx.bank[tt * 2 + dh]
                p.mm(ps_y, CF[0:16, t0 + tt * 128:t0 + (tt + 1) * 128],
                     be2sb[0:16, dh * 512:(dh + 1) * 512], start=True, stop=True)
                p.cp(y_acc[tt][:, dh * 512:(dh + 1) * 512], ps_y)
        for e in range(16):
            for ic in range(4):
                ps1 = ctx.bank[0 + (ic % 2) * 2]
                ps3 = ctx.bank[1 + (ic % 2) * 2]
                s1 = w1r[ic % 2]
                s3 = w3r[ic % 2]
                p.dma("DG", s1, ctx.din["L1_ffn_We1"][e].rearrange(
                    "(kc kp2) i -> kp2 kc i", kp2=128)[
                    :, :, ic * 128:(ic + 1) * 128])
                p.dma("DG", s3, ctx.din["L1_ffn_We3"][e].rearrange(
                    "(kc kp2) i -> kp2 kc i", kp2=128)[
                    :, :, ic * 128:(ic + 1) * 128])
                for k in range(NCH):
                    p.mm(ps1, s1[:, k * 128:(k + 1) * 128],
                         ctx.xn(k, t0, 512), start=(k == 0), stop=(k == 7))
                for k in range(NCH):
                    p.mm(ps3, s3[:, k * 128:(k + 1) * 128],
                         ctx.xn(k, t0, 512), start=(k == 0), stop=(k == 7))
                p.act(t_si, ps1, AF.Silu, bias=ctx.bc("be1", e * 4 + ic))
                p.stt(H[ic], ps3, ctx.bc("be3", e * 4 + ic), t_si,
                      ALU.add, ALU.mult)
            for ic in range(4):
                p.dma("DG", w2r[0],
                      ctx.din["L1_ffn_We2"][e][ic * 128:(ic + 1) * 128, :])
                for tt in range(4):
                    for dh in range(2):
                        ps_y = ctx.bank[tt * 2 + dh]
                        p.mm(ps_y, H[ic][:, tt * 128:(tt + 1) * 128],
                             w2r[0][:, dh * 512:(dh + 1) * 512],
                             start=(ic == 0), stop=(ic == 3))
            for tt in range(4):
                comb_col = COMB[:, (s * 4 + tt) * 16 + e:(s * 4 + tt) * 16 + e + 1]
                for dh in range(2):
                    ps_y = ctx.bank[tt * 2 + dh]
                    p.stt(y_acc[tt][:, dh * 512:(dh + 1) * 512], ps_y, comb_col,
                          y_acc[tt][:, dh * 512:(dh + 1) * 512],
                          ALU.mult, ALU.add)
        for icg in range(8):
            ps1 = ctx.bank[0 + (icg % 2) * 2]
            ps3 = ctx.bank[1 + (icg % 2) * 2]
            s1 = w1r[icg % 2]
            s3 = w3r[icg % 2]
            p.dma("DG", s1, ctx.din["L1_ffn_ws1"].rearrange(
                "(kc kp2) i -> kp2 kc i", kp2=128)[
                :, :, icg * 128:(icg + 1) * 128])
            p.dma("DG", s3, ctx.din["L1_ffn_ws3"].rearrange(
                "(kc kp2) i -> kp2 kc i", kp2=128)[
                :, :, icg * 128:(icg + 1) * 128])
            for k in range(NCH):
                p.mm(ps1, s1[:, k * 128:(k + 1) * 128],
                     ctx.xn(k, t0, 512), start=(k == 0), stop=(k == 7))
            for k in range(NCH):
                p.mm(ps3, s3[:, k * 128:(k + 1) * 128],
                     ctx.xn(k, t0, 512), start=(k == 0), stop=(k == 7))
            p.act(t_si, ps1, AF.Silu, bias=ctx.bc("bs1", icg))
            p.stt(HS[icg], ps3, ctx.bc("bs3", icg), t_si, ALU.add, ALU.mult)
        for icg in range(8):
            p.dma("DG", w2r[0],
                  ctx.din["L1_ffn_ws2"][icg * 128:(icg + 1) * 128, :])
            for tt in range(4):
                for dh in range(2):
                    ps_y = ctx.bank[tt * 2 + dh]
                    p.mm(ps_y, HS[icg][:, tt * 128:(tt + 1) * 128],
                         w2r[0][:, dh * 512:(dh + 1) * 512],
                         start=(icg == 0), stop=False)
        for tt in range(4):
            for dh in range(2):
                ps_y = ctx.bank[tt * 2 + dh]
                p.mm(ps_y, ctx.ones1, bs2row[0:1, dh * 512:(dh + 1) * 512],
                     start=False, stop=True)
                p.tt(y_acc[tt][:, dh * 512:(dh + 1) * 512],
                     y_acc[tt][:, dh * 512:(dh + 1) * 512], ps_y, ALU.add)
        for tt in range(4):
            for dc in range(8):
                ps_tr = ctx.bank[4 + dc % 2]
                p.transp(ps_tr[:, 0:128], y_acc[tt][:, dc * 128:(dc + 1) * 128],
                         ctx.ident_f32)
                xs = ctx.xres(dc, t0 + tt * 128, 128)
                p.tt(xs, xs, ps_tr[:, 0:128], ALU.add)
    p.barrier()


def emit_l1_attn(ctx):
    p = ctx.p
    a = ctx.arena
    a.reset()
    KB = [a.alloc([128, T], BF16, f"lk{i}") for i in range(2)]
    VB = [a.alloc([128, 256], BF16, f"lv{i}") for i in range(16)]
    QB = [a.alloc([128, 128], BF16, f"lq{i}") for i in range(8)]
    wring = [a.alloc([128, 1024], F32R, f"lw{i}") for i in range(2)]
    t_ke = a.alloc([128, 512], F32R, "lke")
    t_kr = a.alloc([128, 512], F32R, "lkr")
    row_mx = a.alloc([2, 512], F32, "lrm")
    row_rs = a.alloc([2, 512], F32R, "lrr")
    t_sc = a.alloc([128, T], F32, "lsc")
    t_pu = a.alloc([128, T], BF16, "lpu")
    t_pt = [a.alloc([128, 128], BF16, f"lpt{i}") for i in range(4)]
    OB = [a.alloc([128, 128], BF16, f"lo{i}") for i in range(8)]
    wpring = [a.alloc([128, 1024], BF16, f"lwp{i}") for i in range(2)]
    sums = a.alloc([128, 8], F32, "lsm")
    rsum = a.alloc([128, 2], F32, "lrs")
    negmx = a.alloc([128, 2], F32, "lnm")
    w_attn = ctx.din["L1_attn_w_attn"]
    w_proj = ctx.din["L1_attn_w_proj"]

    def rope_rms(ps, dst, t0, n):
        ke = t_ke[:, 0:n]
        kr = t_kr[:, 0:n]
        p.act(ke, ps, AF.Identity)
        ps_rot = ctx.bank[6]
        p.mm(ps_rot[:, 0:n], ctx.rl1, ke, start=True, stop=True)
        p.tt(kr, ke, ctx.l1_cos[:, t0:t0 + n], ALU.mult)
        p.tt(ke, ps_rot[:, 0:n], ctx.l1_sin[:, t0:t0 + n], ALU.mult)
        p.tt(kr, kr, ke, ALU.add)
        p.act(ke, kr, AF.Square)
        ps_ss = ctx.bank[7][0:2, 0:n]
        p.mm(ps_ss, ctx.bo2, ke, start=True, stop=True)
        p.act(row_mx[:, 0:n], ps_ss, AF.Ln, scale=ctx.cc(1.0 / 64),
              bias=ctx.cc(1e-6))
        p.act(row_rs[:, 0:n], row_mx[:, 0:n], AF.Exp, scale=ctx.cc(-0.5))
        ps_bc = ctx.bank[5]
        p.mm(ps_bc[:, 0:n], ctx.bc2, row_rs[:, 0:n], start=True, stop=True)
        p.tt(dst, kr, ps_bc[:, 0:n], ALU.mult)

    for mc in range(4):
        slab = wring[mc % 2]
        src = w_attn.rearrange("(kc kp) m -> kp kc m", kp=128)[
            :, :, 1024 + mc * 128:1024 + (mc + 1) * 128]
        p.dma("DG", slab, src)
        for s in range(4):
            t0 = s * 512
            ps = ctx.bank[(mc * 4 + s) % 4]
            for k in range(NCH):
                p.mm(ps, slab[:, k * 128:(k + 1) * 128], ctx.xn(k, t0, 512),
                     start=(k == 0), stop=(k == NCH - 1))
            if mc < 2:
                rope_rms(ps, KB[mc][:, t0:t0 + 512], t0, 512)
            else:
                vc = mc - 2
                p.act(t_pu[:, 0:512], ps, AF.Identity)
                for j in range(4):
                    pst = ctx.bank[7].bitcast(BF16)[:, 256:384]
                    p.transp(pst, t_pu[:, j * 128:(j + 1) * 128], ctx.ident_bf)
                    p.cp(VB[s * 4 + j][:, vc * 128:(vc + 1) * 128], pst, eng="S")

    wv = w_attn.rearrange("(kc kp) m -> kp kc m", kp=128)
    for qt in range(16):
        q0 = qt * 128
        for mc in range(8):
            A = (mc % 4) + 8 * (mc // 4)
            B = A + 4
            slab = wring[mc % 2]
            sv = slab.rearrange("p (kc m) -> p kc m", m=128)
            p.dma("DG", sv[:, :, 0:64], wv[:, :, A * 64:(A + 1) * 64], ch=slab.tensor.name)
            p.dma("DG", sv[:, :, 64:128], wv[:, :, B * 64:(B + 1) * 64], ch=slab.tensor.name + "b")
            ps = ctx.bank[mc % 4]
            for k in range(NCH):
                p.mm(ps[:, 0:128], slab[:, k * 128:(k + 1) * 128],
                     ctx.xn(k, q0, 128), start=(k == 0), stop=(k == NCH - 1))
            rope_rms(ps[:, 0:128], QB[mc], q0, 128)
        kspan = (qt + 1) * 128
        nblk = cdiv(kspan, 512)
        for h in range(16):
            hc, pb = (h % 4) + 4 * (h // 8), ((h // 4) % 2) * 64
            g = h // 4
            kc_, kpb = g // 2, (g % 2) * 64
            for b in range(nblk):
                bw = min(512, kspan - b * 512)
                ps_sc = ctx.bank[h % 2]
                p.mm(ps_sc[:, 0:bw], QB[hc][pb:pb + 64, :],
                     KB[kc_][kpb:kpb + 64, b * 512:b * 512 + bw],
                     start=True, stop=True)
                if b == nblk - 1:
                    sc = t_sc[:, b * 512:b * 512 + bw]
                    p.stt(sc, ps_sc[:, 0:bw], 0.125,
                          ctx.mask_l1[:, 512 - bw:512], ALU.mult, ALU.add)
                else:
                    p.ts(t_sc[:, b * 512:b * 512 + bw], ps_sc[:, 0:bw], 0.125,
                         ALU.mult)
            p.red(negmx[:, 0:1], t_sc[:, 0:kspan], ALU.max, negate=True)
            p.act(t_pu[:, 0:kspan], t_sc[:, 0:kspan], AF.Exp,
                  bias=negmx[:, 0:1], accum=sums[:, 0:1])
            p.act(sums[:, 4:5], sums[:, 0:1], AF.Ln)
            p.act(rsum[:, 0:1], sums[:, 4:5], AF.Exp, scale=ctx.cc(-1.0))
            p.ts(t_pu[:, 0:kspan], t_pu[:, 0:kspan], rsum[:, 0:1], ALU.mult,
                 eng="P")
            ps_o = ctx.bank[4 + h % 2][0:64, 0:128]
            ptb = ctx.bank[2 + h % 2].bitcast(BF16)
            for kc0 in range(0, qt + 1, 8):
                ke = min(kc0 + 8, qt + 1)
                for kc in range(kc0, ke):
                    p.transp(ptb[:, (kc % 8) * 128:(kc % 8 + 1) * 128],
                             t_pu[:, kc * 128:(kc + 1) * 128], ctx.ident_bf)
                for kc in range(kc0, ke):
                    p.cp(t_pt[kc % 4], ptb[:, (kc % 8) * 128:(kc % 8 + 1) * 128])
                    p.mm(ps_o, VB[kc][:, g * 64:(g + 1) * 64], t_pt[kc % 4],
                         start=(kc == 0), stop=(kc == qt))
            p.act(OB[hc][pb:pb + 64, :], ps_o, AF.Identity)
        for oc in range(8):
            A = (oc % 4) + 8 * (oc // 4)
            B = A + 4
            slab = wpring[oc % 2]
            p.dma("DG", slab[0:64, :], w_proj[A * 64:(A + 1) * 64, :],
                  ch=slab.tensor.name)
            p.dma("DG", slab[64:128, :], w_proj[B * 64:(B + 1) * 64, :],
                  ch=slab.tensor.name + "b")
            for m in range(8):
                psw = ctx.bank[m][:, 0:128]
                p.mm(psw, slab[:, m * 128:(m + 1) * 128], OB[oc],
                     start=(oc == 0), stop=(oc == 7))
                if oc == 7:
                    xs = ctx.xres(m, q0, 128)
                    p.tt(xs, xs, psw, ALU.add)
    p.barrier()


# version 11 (final)
# speedup vs baseline: 1.1746x; 1.1746x over previous
import numpy as np
import concourse.bass as bass
import concourse.mybir as mybir
from concourse.bass_utils import run_bass_kernel_spmd

F32 = mybir.dt.float32
F32R = mybir.dt.float32r
BF16 = mybir.dt.bfloat16
U8 = mybir.dt.uint8
AF = mybir.ActivationFunctionType
ALU = mybir.AluOpType
AX = mybir.AxisListType.X

T = 2048
D = 1024
DEBUG_L0A = False
DEBUG_MOE = False
FFN_ONLY = None
NCH = 8
PITCH = 2052
GOFF = 4

COMPUTE_OF = {"T": "T", "V": "V", "S": "S", "P": "G", "DQ": "Q", "DG": "G"}
ENGINE_STREAMS = {"T", "V", "S", "P"}


class Prog:

    def __init__(self, nc):
        self.nc = nc
        self.sem = {k: nc.semaphore("sem" + k).__enter__()
                    for k in ("T", "V", "S", "P")}
        self.cnt = {k: 0 for k in self.sem}
        self.clock = {e: {} for e in ("T", "V", "S", "G", "Q")}
        self.tiles = {}

    def _dma_ch(self, ch):
        key = "D_" + ch
        if key not in self.sem:
            self.sem[key] = self.nc.semaphore(key).__enter__()
            self.cnt[key] = 0
        return key

    def eng(self, e):
        nc = self.nc
        return {"T": nc.tensor, "V": nc.vector, "S": nc.scalar,
                "G": nc.gpsimd, "Q": nc.sync}[e]

    def _tile(self, name):
        t = self.tiles.get(name)
        if t is None:
            t = self.tiles[name] = {"w": None, "r": {}}
        return t

    def _wait(self, cons, milestones):
        eng = self.eng(cons)
        cl = self.clock[cons]
        for (sk, v) in milestones:
            if sk in ENGINE_STREAMS and COMPUTE_OF[sk] == cons:
                continue
            if cl.get(sk, 0) >= v:
                continue
            eng.wait_ge(self.sem[sk], v)
            cl[sk] = v

    def emit(self, stream, build, reads=(), writes=()):
        cons = COMPUTE_OF[stream]
        ms = []
        for ap in reads:
            t = self._tile(ap.tensor.name)
            if t["w"]:
                ms.append(t["w"])
        for ap in writes:
            t = self._tile(ap.tensor.name)
            if t["w"]:
                ms.append(t["w"])
            ms.extend(t["r"].items())
        self._wait(cons, ms)
        inst = build(self.eng(cons))
        self.cnt[stream] += 1
        inst.then_inc(self.sem[stream], 1)
        m = (stream, self.cnt[stream])
        for ap in reads:
            r = self._tile(ap.tensor.name)["r"]
            r[m[0]] = m[1]
        for ap in writes:
            t = self._tile(ap.tensor.name)
            t["w"] = m
            t["r"] = {}
        return m

    def barrier(self):
        snap = dict(self.cnt)
        for cons in ("T", "V", "S", "G", "Q"):
            eng = self.eng(cons)
            cl = self.clock[cons]
            for sk, v in snap.items():
                if v == 0:
                    continue
                if sk in ENGINE_STREAMS and COMPUTE_OF[sk] == cons:
                    continue
                if cl.get(sk, 0) >= v:
                    continue
                eng.wait_ge(self.sem[sk], v)
                cl[sk] = v
        self.tiles = {}

    def dma(self, stream, out, in_, ch=None):
        qcons = COMPUTE_OF[stream]
        if ch is None:
            nm = out.tensor.name
            if "__" in nm:
                ch = nm.split("__")[0]
            else:
                parts = nm.split("_")
                while len(parts) > 1 and parts[-1].isdigit():
                    parts.pop()
                ch = "_".join(parts)
        key = self._dma_ch(ch)
        ms = []
        ti = self._tile(in_.tensor.name)
        if ti["w"]:
            ms.append(ti["w"])
        to = self._tile(out.tensor.name)
        if to["w"]:
            ms.append(to["w"])
        ms.extend(to["r"].items())
        self._wait(qcons, ms)
        inst = self.eng(qcons).dma_start(out, in_)
        self.cnt[key] += 16
        inst.then_inc(self.sem[key], 16)
        m = (key, self.cnt[key])
        self._tile(in_.tensor.name)["r"][key] = self.cnt[key]
        to = self._tile(out.tensor.name)
        to["w"] = m
        to["r"] = {}
        return m

    def mm(self, ps, lhsT, rhs, start, stop):
        return self.emit("T", lambda e: e.matmul(
            ps, lhsT, rhs, start=start, stop=stop, skip_group_check=True),
            reads=[lhsT, rhs], writes=[ps])

    def transp(self, ps, in_, ident):
        return self.emit("T", lambda e: e.transpose(ps, in_, ident),
                         reads=[in_, ident], writes=[ps])

    def tt(self, out, a, b, op, eng="V"):
        return self.emit(eng, lambda e: e.tensor_tensor(out, a, b, op),
                         reads=[a, b], writes=[out])

    def ts(self, out, a, s1, op, s2=None, op2=None, eng="V", accum=None):
        np_ = out.shape[0]
        if isinstance(s1, bass.AP) and s1.shape[0] != np_:
            s1 = s1[0:np_, :]
        if isinstance(s2, bass.AP) and s2.shape[0] != np_:
            s2 = s2[0:np_, :]
        rd = [a] + ([s1] if isinstance(s1, bass.AP) else []) \
            + ([s2] if isinstance(s2, bass.AP) else [])
        wr = [out] + ([accum] if accum is not None else [])
        if s2 is None and op2 is None:
            return self.emit(eng, lambda e: e.tensor_single_scalar(out, a, s1, op),
                             reads=rd, writes=wr)
        return self.emit(eng, lambda e: e.tensor_scalar(
            out, a, s1, s2, op, op2, accum_out=accum), reads=rd, writes=wr)

    def stt(self, out, a, scalar, b, op0, op1, eng="V"):
        if isinstance(scalar, bass.AP) and scalar.shape[0] != out.shape[0]:
            scalar = scalar[0:out.shape[0], :]
        rd = [a, b] + ([scalar] if isinstance(scalar, bass.AP) else [])
        return self.emit(eng, lambda e: e.scalar_tensor_tensor(
            out, a, scalar, b, op0, op1), reads=rd, writes=[out])

    def act(self, out, a, func, bias=None, scale=None, accum=None):
        kw = {}
        rd = [a]
        np_ = out.shape[0]
        if isinstance(bias, bass.AP) and bias.shape[0] != np_:
            bias = bias[0:np_, :]
        if isinstance(scale, bass.AP) and scale.shape[0] != np_:
            scale = scale[0:np_, :]
        if bias is not None:
            kw["bias"] = bias
            if isinstance(bias, bass.AP):
                rd.append(bias)
        if scale is not None:
            kw["scale"] = scale
            if isinstance(scale, bass.AP):
                rd.append(scale)
        wr = [out]
        if accum is not None:
            kw["accum_out"] = accum
            wr.append(accum)
        return self.emit("S", lambda e: e.activation(out, a, func, **kw),
                         reads=rd, writes=wr)

    def cp(self, out, a, eng="V"):
        if eng == "S":
            return self.emit("S", lambda e: e.copy(out, a), reads=[a], writes=[out])
        return self.emit(eng, lambda e: e.tensor_copy(out, a), reads=[a], writes=[out])

    def red(self, out, a, op, negate=False, eng="V"):
        return self.emit(eng, lambda e: e.tensor_reduce(
            out, a, AX, op, negate=negate), reads=[a], writes=[out])

    def recip(self, out, a):
        self.eng("V").wait_ge(self.sem["V"], self.cnt["V"])
        m = self.emit("V", lambda e: e.reciprocal(out, a), reads=[a], writes=[out])
        self.eng("V").wait_ge(self.sem["V"], self.cnt["V"])
        return m

    def scan(self, out, c, v, init, eng="P"):
        rd = [c, v] + ([init] if isinstance(init, bass.AP) else [])
        return self.emit(eng, lambda e: e.tensor_tensor_scan(
            out, c, v, init, ALU.mult, ALU.add), reads=rd, writes=[out])

    def memset(self, ap, val, eng="V"):
        return self.emit(eng, lambda e: e.memset(ap, val), writes=[ap])

    def select(self, out, mask, tru, fls):
        return self.emit("V", lambda e: e.select(out, mask, tru, fls),
                         reads=[mask, tru, fls], writes=[out])


class Arena:

    def __init__(self, nc, base, size):
        self.nc = nc
        self.base = base
        self.size = size
        self.off = 0
        self.n = 0

    def reset(self):
        self.off = 0

    def mark(self):
        return self.off

    def reset_to(self, m):
        self.off = m

    def alloc(self, shape, dtype, tag=""):
        nb = int(np.prod(shape[1:])) * mybir.dt.size(dtype)
        nb = (nb + 31) & ~31
        assert self.off + nb <= self.size, \
            f"arena overflow: {tag} needs {nb} at {self.off}/{self.size}"
        self.n += 1
        h = self.nc.alloc_sbuf_tensor_at(
            f"ar_{tag}__{self.n}", list(shape), dtype, offset=self.base + self.off)
        self.off += nb
        return h.ap()


def cdiv(a, b):
    return (a + b - 1) // b



def _np_consts():
    import ml_dtypes
    bf = ml_dtypes.bfloat16
    C = {}
    C["ident_f32"] = np.eye(128, dtype=np.float32)
    C["ident_bf"] = np.eye(128, dtype=np.float32).astype(bf)

    R = np.zeros((128, 128), dtype=np.float32)
    for base in (0, 64):
        for i in range(32):
            R[base + i, base + i + 32] = -1.0
    RT = np.zeros((128, 128), dtype=np.float32)
    for base in (0, 64):
        for i in range(32):
            RT[base + i + 32, base + i] = -1.0
            RT[base + i, base + i + 32] = 1.0
    C["rl0"] = RT

    RT1 = np.zeros((128, 128), dtype=np.float32)
    for base in (0, 64):
        for i in range(32):
            RT1[base + 2 * i + 1, base + 2 * i] = -1.0
            RT1[base + 2 * i, base + 2 * i + 1] = 1.0
    C["rl1"] = RT1

    half = np.arange(0, 64, 2, dtype=np.float64)
    inv_freq = 1.0 / 10000.0 ** (half / 64.0)
    t_ar = np.arange(T, dtype=np.float64)
    freqs = t_ar[:, None] * inv_freq[None, :]
    freqs = np.concatenate([freqs, freqs], -1)
    base_scale = (half + 0.4 * 64.0) / (1.4 * 64.0)
    power = (t_ar - T // 2) / 128.0
    scale = base_scale[None, :] ** power[:, None]
    scale = np.concatenate([scale, scale], -1)
    cosf, sinf = np.cos(freqs), np.sin(freqs)

    def dup(a):
        return np.tile(np.ascontiguousarray(a.T), (2, 1)).astype(np.float32)
    C["l0_qcos"] = dup(cosf * scale).astype(bf)
    C["l0_qsin"] = dup(sinf * scale).astype(bf)
    C["l0_kcos"] = dup(cosf / scale).astype(bf)
    C["l0_ksin"] = dup(sinf / scale).astype(bf)

    inv1 = 1.0 / 10000.0 ** (np.arange(0, 64, 2, dtype=np.float64) / 64.0)
    f1 = t_ar[:, None] * inv1[None, :]
    emb = np.repeat(f1, 2, axis=-1)
    C["l1_cos"] = dup(np.cos(emb)).astype(bf)
    C["l1_sin"] = dup(np.sin(emb)).astype(bf)

    i_ar = np.arange(128)[:, None]
    j_ar = np.arange(384)[None, :]
    m = np.zeros((128, 384), dtype=np.float32)
    m[(j_ar < 128) & (j_ar < i_ar)] = -1e30
    m[(j_ar >= 256) & (j_ar - 256 > i_ar)] = -1e30
    C["mask_l0"] = m

    j_ar = np.arange(896)[None, :]
    m1 = np.zeros((128, 896), dtype=np.float32)
    m1[(j_ar >= 384) & (j_ar < 512) & (j_ar - 384 > i_ar)] = -1e30
    m1[:, 512:] = -1e30
    C["mask_l1"] = m1

    C["ones_row"] = np.ones((1, 512), dtype=np.float32)
    C["ones_row_bf"] = np.ones((1, 512), dtype=np.float32).astype(bf)
    C["ones_col"] = np.ones((128, 1), dtype=np.float32)
    bo2 = np.zeros((128, 2), dtype=np.float32)
    bo2[0:64, 0] = 1.0
    bo2[64:128, 1] = 1.0
    C["bo2"] = bo2
    bc2 = np.zeros((2, 128), dtype=np.float32)
    bc2[0, 0:64] = 1.0
    bc2[1, 64:128] = 1.0
    C["bc2"] = bc2
    C["ones1"] = np.ones((1, 128), dtype=np.float32)

    ccvals = CCOL_VALS
    C["ccols"] = np.tile(np.asarray(ccvals, dtype=np.float32), (128, 1))
    return C


CCOL_VALS = [1.0, -1.0, -0.5, 1.0 / 64, 1.0 / 1024, 1e-5, 1e-6, 0.5, -8.0, 0.125]
CCOL_IDX = {v: i for i, v in enumerate(CCOL_VALS)}



def _bcol_layout():
    L = {}
    c = 0

    def add(name, n):
        nonlocal c
        L[name] = (c, n)
        c += n
    add("ab1", 48)
    add("ab2", 8)
    add("init_h", 16)
    add("ln_g", 48)
    add("ln_b", 48)
    add("qs", 6)
    add("ks", 6)
    add("fb1", 6 * 44)
    add("fb2", 6 * 8)
    add("bs1", 8)
    add("bs3", 8)
    add("be1", 64)
    add("be3", 64)
    L["_total"] = c
    return L


BCOL = _bcol_layout()
NDERIV = 48 + 16


def _pack_vec_cols(v, nch):
    out = np.zeros((128, nch), dtype=np.float32)
    v = np.asarray(v, dtype=np.float32).ravel()
    for c in range(nch):
        seg = v[c * 128:(c + 1) * 128]
        out[:len(seg), c] = seg
    return out


def _host_biascol(inp):
    nb = BCOL["_total"]
    blob = np.zeros((128, nb), dtype=np.float32)

    def put(name, arr):
        s, n = BCOL[name]
        assert arr.shape == (128, n), (name, arr.shape)
        blob[:, s:s + n] = arr
    put("ab1", _pack_vec_cols(inp["L0_attn_b1"], 48))
    put("ab2", _pack_vec_cols(inp["L0_attn_b2"], 8))
    put("init_h", _pack_vec_cols(inp["L0_attn_init_h"], 16))
    put("ln_g", np.concatenate([_pack_vec_cols(inp["L0_ffn_ln_g"][i], 8)
                                for i in range(6)], 1))
    put("ln_b", np.concatenate([_pack_vec_cols(inp["L0_ffn_ln_b"][i], 8)
                                for i in range(6)], 1))
    put("qs", np.stack([np.tile(inp["L0_ffn_q_scale"][i], 2)
                        for i in range(6)], 1).astype(np.float32))
    put("ks", np.stack([np.tile(inp["L0_ffn_k_scale"][i], 2)
                        for i in range(6)], 1).astype(np.float32))
    fb1 = []
    for i in range(6):
        b = np.asarray(inp["L0_ffn_b1"][i])
        fb1.append(_pack_vec_cols(b[:2730], 22))
        fb1.append(_pack_vec_cols(b[2730:], 22))
    put("fb1", np.concatenate(fb1, 1))
    put("fb2", np.concatenate([_pack_vec_cols(inp["L0_ffn_b2"][i], 8)
                               for i in range(6)], 1))
    put("bs1", _pack_vec_cols(inp["L1_ffn_bs1"], 8))
    put("bs3", _pack_vec_cols(inp["L1_ffn_bs3"], 8))
    put("be1", np.concatenate([_pack_vec_cols(inp["L1_ffn_be1"][e], 4)
                               for e in range(16)], 1))
    put("be3", np.concatenate([_pack_vec_cols(inp["L1_ffn_be3"][e], 4)
                               for e in range(16)], 1))
    return blob



INPUT_SPECS = [
    ("L0_attn_W1", [1024, 6144]), ("L0_attn_W2", [2048, 1024]),
    ("L0_ffn_Wqkv", [6, 1024, 3072]), ("L0_ffn_Wo", [6, 1024, 1024]),
    ("L0_ffn_W1", [6, 1024, 5460]), ("L0_ffn_W2", [6, 2730, 1024]),
    ("L1_attn_w_attn", [1024, 1536]), ("L1_attn_w_proj", [1024, 1024]),
    ("L1_ffn_We1", [16, 1024, 512]), ("L1_ffn_We2", [16, 512, 1024]),
    ("L1_ffn_We3", [16, 1024, 512]),
    ("L1_ffn_ws1", [1024, 1024]), ("L1_ffn_ws2", [1024, 1024]),
    ("L1_ffn_ws3", [1024, 1024]),
    ("L1_ffn_be2", [16, 1024]), ("L1_ffn_bs2", [1024]),
    ("L1_ffn_gate_b", [16]),
    ("xT", [1024, 2048]), ("biascol", [128, BCOL["_total"]]),
    ("gate_wT", [1024, 16]), ("gateb_tile", [128, 16]),
]


class Ctx:

    def __init__(self, load_xres, load_xn, store_state, np_consts):
        nc = bass.Bass("TRN2", target_bir_lowering=False)
        self.nc = nc
        self.p = Prog(nc)
        p = self.p

        self.din = {}
        for name, shape in INPUT_SPECS:
            self.din[name] = nc.dram_tensor(name, list(shape), F32,
                                            kind="ExternalInput").ap()
        self.y_out = nc.dram_tensor("y", [D, T], F32, kind="ExternalOutput").ap()
        self.load_xres = load_xres
        self.load_xn = load_xn
        self.store_state = store_state
        if load_xres:
            self.xres_in = nc.dram_tensor("xres_in", [D, T], F32,
                                          kind="ExternalInput").ap()
        if load_xn:
            self.xn_in = nc.dram_tensor("xn_in", [D, T], F32,
                                        kind="ExternalInput").ap()
        if store_state:
            self.xres_st = nc.dram_tensor("xres_st", [D, T], F32,
                                          kind="ExternalOutput").ap()
            self.xn_st = nc.dram_tensor("xn_st", [D, T], F32,
                                        kind="ExternalOutput").ap()
        self.xres_spill = nc.dram_tensor("xres_spill", [D, T], F32,
                                         kind="Internal").ap()
        self.dconst = {k: nc.inline_tensor(v, name="c_" + k).ap()
                       for k, v in np_consts.items()}

        base = (nc.sbuf_base + 31) & ~31
        off = base

        def place(nbytes):
            nonlocal off
            o = off
            off += (nbytes + 31) & ~31
            return o
        self.o_xn = place(NCH * PITCH * 4)
        self.o_biascol = place((BCOL["_total"] + NDERIV) * 4)
        o_core = place(5056)
        self.o_lset = place(18944)
        self.o_xres = place(NCH * T * 4)
        arena_base = off
        arena_size = nc.sbuf_top - arena_base
        assert arena_size >= 50000, f"arena too small: {arena_size}"
        self.arena = Arena(nc, arena_base, arena_size)
        self.arena_l0ffn = Arena(nc, self.o_xres, (arena_base - self.o_xres)
                                 + arena_size)

        self.XN = nc.alloc_sbuf_tensor_at("XN", [128, NCH * PITCH], F32R,
                                          offset=self.o_xn).ap()
        self.XRES = nc.alloc_sbuf_tensor_at("XRES", [128, NCH * T], F32R,
                                            offset=self.o_xres).ap()
        self.bcol = nc.alloc_sbuf_tensor_at(
            "bcol", [128, BCOL["_total"] + NDERIV], F32, offset=self.o_biascol).ap()

        co = o_core

        def cplace(name, shape, dtype):
            nonlocal co
            h = nc.alloc_sbuf_tensor_at("k_" + name, list(shape), dtype, offset=co)
            co += (int(np.prod(shape[1:])) * mybir.dt.size(dtype) + 31) & ~31
            return h.ap()
        self.ident_f32 = cplace("idf", [128, 128], F32)
        self.ident_bf = cplace("idb", [128, 128], BF16)
        self.ones_row = cplace("onr", [1, 512], F32R)
        self.ones_row_bf = cplace("onrb", [1, 512], BF16)
        self.ones_col = cplace("onc", [128, 1], F32R)
        self.bo2 = cplace("bo2", [128, 2], F32R)
        self.bc2 = cplace("bc2", [2, 128], F32R)
        self.ones1 = cplace("on1", [1, 128], F32R)
        self.ccols = cplace("ccl", [128, len(CCOL_VALS)], F32)
        self.gih = cplace("gih", [128, 16], F32)
        assert co <= o_core + 5056, co

        lo = self.o_lset

        def lplace(name, shape, dtype):
            nonlocal lo
            h = nc.alloc_sbuf_tensor_at("k_" + name, list(shape), dtype, offset=lo)
            lo += (int(np.prod(shape[1:])) * mybir.dt.size(dtype) + 31) & ~31
            return h.ap()
        self.l0_qcos = lplace("qc", [128, T], BF16)
        self.l0_qsin = lplace("qs_", [128, T], BF16)
        self.l0_kcos = lplace("kc", [128, T], BF16)
        self.l0_ksin = lplace("ks_", [128, T], BF16)
        self.mask_l0 = lplace("ml0", [128, 384], F32)
        self.rl0 = lplace("rl0", [128, 128], F32R)
        assert lo <= self.o_lset + 18944, lo
        lo = self.o_lset
        self.l1_cos = lplace("l1c", [128, T], BF16)
        self.l1_sin = lplace("l1s", [128, T], BF16)
        self.mask_l1 = lplace("ml1", [128, 896], F32)
        self.rl1 = lplace("rl1", [128, 128], F32R)
        self.gateb = lplace("gtb", [128, 16], F32)
        self.be2sb = lplace("be2", [16, 1024], BF16)
        self.bs2row = lplace("bs2", [1, 1024], F32R)
        assert lo <= self.o_lset + 18944, lo

        self.bank = [nc.alloc_psum_tensor(f"bank{i}", [128, 512], F32).ap()
                     for i in range(8)]

    def bc(self, name, i=0):
        s, n = BCOL[name]
        assert i < n, (name, i)
        return self.bcol[:, s + i:s + i + 1]

    def bcr(self, name, i, n):
        s, _ = BCOL[name]
        return self.bcol[:, s + i:s + i + n]

    def negb(self, i):
        return self.bcol[:, BCOL["_total"] + i:BCOL["_total"] + i + 1]

    def cc(self, val):
        return self.ccols[:, CCOL_IDX[val]:CCOL_IDX[val] + 1]

    def xn(self, k, t0=0, n=T, shift=False):
        b = k * PITCH + GOFF + t0 - (1 if shift else 0)
        return self.XN[:, b:b + n]

    def xres(self, k, t0=0, n=T):
        b = k * T + t0
        return self.XRES[:, b:b + n]


def emit_init(ctx, first_launch):
    p, nc = ctx.p, ctx.nc
    for name, ap in (("ident_f32", ctx.ident_f32), ("ident_bf", ctx.ident_bf),
                     ("mask_l0", ctx.mask_l0), ("ccols", ctx.ccols)):
        p.dma("DQ", ap, ctx.dconst[name], ch="consts")
    for name, ap in (("ones_row", ctx.ones_row), ("ones_col", ctx.ones_col),
                     ("bo2", ctx.bo2), ("bc2", ctx.bc2), ("ones1", ctx.ones1),
                     ("rl0", ctx.rl0)):
        p.dma("DG", ap, ctx.dconst[name], ch="gconsts")
    p.dma("DQ", ctx.ones_row_bf, ctx.dconst["ones_row_bf"], ch="consts")
    for name, ap in (("l0_qcos", ctx.l0_qcos), ("l0_qsin", ctx.l0_qsin),
                     ("l0_kcos", ctx.l0_kcos), ("l0_ksin", ctx.l0_ksin)):
        p.dma("DQ", ap, ctx.dconst[name], ch="consts")
    p.dma("DQ", ctx.bcol[:, 0:BCOL["_total"]], ctx.din["biascol"], ch="consts")

    if first_launch:
        for k in range(NCH):
            p.dma("DG", ctx.xres(k), ctx.din["xT"][k * 128:(k + 1) * 128, :])
    elif ctx.load_xres:
        for k in range(NCH):
            p.dma("DG", ctx.xres(k), ctx.xres_in[k * 128:(k + 1) * 128, :])
    if ctx.load_xn:
        for k in range(NCH):
            p.dma("DG", ctx.xn(k), ctx.xn_in[k * 128:(k + 1) * 128, :])

    p.barrier()
    for k in range(NCH):
        b = k * PITCH + GOFF - 1
        p.memset(ctx.XN[:, b:b + 1].bitcast(F32), 0.0)

    s, n = BCOL["ab1"]
    p.ts(ctx.bcol[:, BCOL["_total"]:BCOL["_total"] + 48],
         ctx.bcol[:, s:s + 48], -1.0, ALU.mult)
    ih = ctx.bcr("init_h", 0, 16)
    a = ctx.arena
    a.reset()
    t1 = a.alloc([128, 16], F32, "init1")
    t2 = a.alloc([128, 16], F32, "init2")
    p.act(t1, ih, AF.Exp, scale=ctx.cc(-1.0))
    if DEBUG_L0A:
        p.dma("DQ", ctx.y_out[512:640, 0:16], t1)
    p.ts(t1, t1, 1.0, ALU.add)
    if DEBUG_L0A:
        p.dma("DQ", ctx.y_out[512:640, 16:32], t1)
    p.recip(t2, t1)
    if DEBUG_L0A:
        p.dma("DQ", ctx.y_out[512:640, 32:48], t2)
    p.ts(t2, t2, 0.5, ALU.min)
    if DEBUG_L0A:
        p.dma("DQ", ctx.y_out[512:640, 48:64], t2)
    p.act(t1, ih, AF.Relu)
    if DEBUG_L0A:
        p.dma("DQ", ctx.y_out[512:640, 64:80], t1)
    p.tt(ctx.gih, t1, t2, ALU.add)
    p.barrier()


def emit_store_state(ctx):
    p = ctx.p
    for k in range(NCH):
        p.dma("DQ", ctx.xres_st[k * 128:(k + 1) * 128, :],
              ctx.xres(k).bitcast(F32))
        p.dma("DQ", ctx.xn_st[k * 128:(k + 1) * 128, :],
              ctx.xn(k).bitcast(F32))


def emit_output(ctx):
    p = ctx.p
    for k in range(NCH):
        p.dma("DQ", ctx.y_out[k * 128:(k + 1) * 128, :],
              ctx.xres(k).bitcast(F32))


def emit_finish(ctx):
    p = ctx.p
    for sk, v in p.cnt.items():
        if sk.startswith("D_") and v > 0:
            p.eng("Q").wait_ge(p.sem[sk], v)


def emit_ln(ctx, dst_is_xn=True, affine_layer=None, src_shift=False):
    p = ctx.p
    a = ctx.arena
    a.reset()
    sq = a.alloc([128, T], F32R, "lnsq")
    murow = a.alloc([1, T], F32R, "lnmu")
    rsrow = a.alloc([1, T], F32R, "lnrs")
    trow = a.alloc([1, 512], F32, "lnt")
    tn = a.alloc([128, 512], F32, "lnn")
    for s in range(4):
        t0 = s * 512
        ps_sum = ctx.bank[0][0:1, :]
        ps_ssq = ctx.bank[1][0:1, :]
        for k in range(NCH):
            p.act(sq[:, t0:t0 + 512], ctx.xres(k, t0, 512), AF.Square)
            p.mm(ps_sum, ctx.ones_col, ctx.xres(k, t0, 512),
                 start=(k == 0), stop=(k == NCH - 1))
            p.mm(ps_ssq, ctx.ones_col, sq[:, t0:t0 + 512].bitcast(F32R),
                 start=(k == 0), stop=(k == NCH - 1))
        mu = murow[0:1, t0:t0 + 512]
        rs = rsrow[0:1, t0:t0 + 512]
        p.ts(mu, ps_sum, 1.0 / D, ALU.mult)
        p.tt(trow, mu, mu, ALU.mult)
        p.ts(trow, trow, -1.0, ALU.mult)
        p.stt(trow, ps_ssq, 1.0 / D, trow, ALU.mult, ALU.add)
        p.act(trow, trow, AF.Ln, bias=ctx.cc(1e-5))
        p.act(rs, trow, AF.Exp, scale=ctx.cc(-0.5))
        ps_bmu = ctx.bank[2]
        ps_brs = ctx.bank[3]
        p.mm(ps_bmu, ctx.ones1, mu, start=True, stop=True)
        p.mm(ps_brs, ctx.ones1, rs, start=True, stop=True)
        for k in range(NCH):
            p.tt(tn, ctx.xres(k, t0, 512), ps_bmu, ALU.subtract)
            p.tt(ctx.xn(k, t0, 512), tn, ps_brs, ALU.mult)
    p.barrier()



def _phase(ctx, name):
    if name == "ln0":
        emit_ln(ctx)
    elif name == "addstream":
        for k in range(NCH):
            ctx.p.tt(ctx.xres(k), ctx.xres(k), ctx.xn(k), ALU.add)
        ctx.p.barrier()
    elif name == "l0attn":
        emit_l0_attn(ctx)
    elif name.startswith("l0ffn"):
        emit_l0_ffn_layer(ctx, int(name[5:]))
    elif name == "ln2":
        emit_ln(ctx)
        emit_l1_consts(ctx)
    elif name == "moe":
        emit_moe(ctx)
    elif name == "l1attn":
        emit_l1_attn(ctx)
    else:
        raise ValueError(name)


ALL_PHASES = ["ln0", "l0attn"] + [f"l0ffn{i}" for i in range(6)] + \
    ["addstream", "ln2", "moe", "l1attn"]

LAUNCH_PLAN = [ALL_PHASES]

_SPILL_PHASES = {f"l0ffn{i}" for i in range(6)}


def build_launch(phases, first, last, np_consts):
    needs_xres = any(ph not in _SPILL_PHASES for ph in phases)
    needs_xn = phases[0] != "ln0"
    ctx = Ctx(load_xres=(not first) and needs_xres,
              load_xn=(not first) and needs_xn,
              store_state=not last, np_consts=np_consts)
    emit_init(ctx, first_launch=first)
    prev_spill = False
    for ph in phases:
        spill = ph in _SPILL_PHASES
        if spill and not prev_spill and needs_xres and len(phases) > 1:
            for k in range(NCH):
                ctx.p.dma("DQ", ctx.xres_spill[k * 128:(k + 1) * 128, :],
                          ctx.xres(k).bitcast(F32))
            ctx.p.barrier()
        if not spill and prev_spill and needs_xres and len(phases) > 1:
            for k in range(NCH):
                ctx.p.dma("DG", ctx.xres(k),
                          ctx.xres_spill[k * 128:(k + 1) * 128, :])
            ctx.p.barrier()
        prev_spill = spill
        _phase(ctx, ph)
    if not last:
        emit_store_state(ctx)
    else:
        emit_output(ctx)
    emit_finish(ctx)
    return ctx


def _host_inputs(inputs, core):
    m = {}
    for name, shape in INPUT_SPECS:
        if name == "xT":
            m[name] = np.ascontiguousarray(inputs["L0_attn_x"][core].T)
        elif name == "biascol":
            m[name] = _host_biascol(inputs)
        elif name == "gate_wT":
            m[name] = np.ascontiguousarray(inputs["L1_ffn_gate_w"].T)
        elif name == "gateb_tile":
            m[name] = np.ascontiguousarray(
                np.tile(np.asarray(inputs["L1_ffn_gate_b"])[None, :], (128, 1)))
        else:
            m[name] = np.asarray(inputs[name])
    return m


def kernel(**inputs):
    np_consts = _np_consts()
    base_maps = [_host_inputs(inputs, core) for core in range(8)]
    state = [dict() for _ in range(8)]
    out = np.empty((8, T, D), dtype=np.float32)
    nl = len(LAUNCH_PLAN)
    for li, phases in enumerate(LAUNCH_PLAN):
        first, last = li == 0, li == nl - 1
        ctx = build_launch(phases, first, last, np_consts)
        in_maps = []
        for core in range(8):
            m = dict(base_maps[core])
            if not first:
                if ctx.load_xres:
                    m["xres_in"] = state[core]["xres"]
                if ctx.load_xn:
                    m["xn_in"] = state[core]["xn"]
            in_maps.append(m)
        res = run_bass_kernel_spmd(ctx.nc, in_maps, core_ids=list(range(8)))
        for core in range(8):
            r = res.results[core]
            if not last:
                if "xres_st" in r:
                    state[core]["xres"] = r["xres_st"]
                state[core]["xn"] = r["xn_st"]
            else:
                out[core] = r["y"].T
        if not last:
            for core in range(8):
                if "xres_st" not in res.results[core] or not ctx.store_state:
                    pass
    return out



def emit_l0_attn(ctx):
    p = ctx.p
    a = ctx.arena
    a.reset()
    W1 = ctx.din["L0_attn_W1"]
    W2 = ctx.din["L0_attn_W2"]
    wring = [a.alloc([128, 1024], F32R, f"aw1_{i}") for i in range(3)]
    w2ring = [a.alloc([128, 1024], BF16, f"aw2_{i}") for i in range(2)]
    t_a = a.alloc([128, 512], F32, "ga")
    t_b = a.alloc([128, 512], F32, "gb")
    t_uw = a.alloc([128, 1024], F32, "guw")
    t_r = a.alloc([128, 1024], F32, "gr")
    CVc = a.alloc([128, T], F32, "cvc")
    CVv = a.alloc([128, T], F32, "cvv")
    OH = [a.alloc([128, T], BF16, f"oh{i}") for i in range(2)]

    for c in range(16):
        for g in range(3):
            src = W1.rearrange("(kc kp) m -> kp kc m", kp=128)[
                :, :, g * 2048 + c * 128: g * 2048 + (c + 1) * 128]
            p.dma("DG", wring[g], src)
        for s in range(4):
            t0 = s * 512
            psb = 0 if s % 2 == 0 else 3
            ps_h, ps_f, ps_i = (ctx.bank[psb], ctx.bank[psb + 1],
                                ctx.bank[psb + 2])
            for g, ps in ((0, ps_h), (1, ps_f), (2, ps_i)):
                for k in range(NCH):
                    p.mm(ps, wring[g][:, k * 128:(k + 1) * 128],
                         ctx.xn(k, t0, 512), start=(k == 0), stop=(k == NCH - 1))
            cslice = CVc[:, t0:t0 + 512]
            vslice = CVv[:, t0:t0 + 512]
            u = t_uw[:, 0:512]
            w = t_uw[:, 512:1024]
            ru = t_r[:, 0:512]
            rw = t_r[:, 512:1024]
            p.act(t_a, ps_f, AF.Exp, scale=ctx.cc(-1.0), bias=ctx.negb(16 + c))
            p.act(t_b, ps_i, AF.Exp, scale=ctx.cc(-1.0), bias=ctx.negb(32 + c))
            p.tt(u, t_a, t_b, ALU.add)
            p.ts(u, u, 2.0, ALU.add)
            p.ts(w, ps_h, ctx.bc("ab1", c), ALU.add, 0.0, ALU.min)
            p.act(w, w, AF.Exp)
            p.ts(w, w, 1.0, ALU.add)
            p.recip(t_r, t_uw)
            p.ts(t_b, t_b, 1.0, ALU.add)
            p.tt(cslice, t_b, ru, ALU.mult)
            p.ts(t_a, t_a, 1.0, ALU.add)
            p.tt(t_b, t_a, ru, ALU.mult)
            p.ts(w, w, -1.0, ALU.add)
            p.tt(w, w, rw, ALU.mult)
            p.act(t_a, ps_h, AF.Relu, bias=ctx.bc("ab1", c))
            p.tt(w, t_a, w, ALU.add)
            p.tt(vslice, t_b, w, ALU.mult)
        oh = OH[c % 2]
        p.scan(oh, CVc, CVv, ctx.gih[:, c:c + 1], eng="V")
        if c == 0 and DEBUG_L0A:
            p.dma("DQ", ctx.y_out[0:128, 0:1024], oh.bitcast(F32)[:, 0:1024])
            p.dma("DQ", ctx.y_out[128:256, :], CVc)
            p.dma("DQ", ctx.y_out[256:384, :], CVv)
            p.dma("DQ", ctx.y_out[384:512, 0:16], ctx.gih)
        if c % 2 == 1:
            ee = c // 2
            for j in range(2):
                cc = c - 1 + j
                p.dma("DG", w2ring[j],
                      W2[cc * 128:(cc + 1) * 128, :])
            for m in range(NCH):
                for s in range(4):
                    t0 = s * 512
                    psw = ctx.bank[6 + (m * 4 + s) % 2]
                    for j in range(2):
                        p.mm(psw, w2ring[j][:, m * 128:(m + 1) * 128],
                             OH[j][:, t0:t0 + 512], start=(j == 0), stop=(j == 1))
                    if ee < 7:
                        p.tt(ctx.xres(m, t0, 512), ctx.xres(m, t0, 512), psw,
                             ALU.add)
                    else:
                        p.stt(ctx.xres(m, t0, 512), psw, ctx.bc("ab2", m),
                              ctx.xres(m, t0, 512), ALU.add, ALU.add)
    p.barrier()



def emit_l0_ffn_layer(ctx, li):
    p = ctx.p
    a = ctx.arena_l0ffn

    Wqkv = ctx.din["L0_ffn_Wqkv"][li]
    Wo = ctx.din["L0_ffn_Wo"][li]
    W1 = ctx.din["L0_ffn_W1"][li]
    W2 = ctx.din["L0_ffn_W2"][li]

    a.reset()
    skip_attn = bool(FFN_ONLY)
    wring = [a.alloc([128, 1024], F32R, f"fw{i}") for i in range(2)]
    QBUF = [a.alloc([128, 1024], BF16, f"fq{i}") for i in range(8)]
    KBUF = [a.alloc([128, 1280], BF16, f"fk{i}") for i in range(8)]
    VBUF = [a.alloc([128, 1024], BF16, f"fv{i}") for i in range(10)]
    OBUF = [a.alloc([128, 1024], BF16, f"fo{i}") for i in range(8)]
    t_sq = a.alloc([128, 512], F32R, "fsq")
    t_qe = a.alloc([128, 512], F32R, "fqe")
    t_qn = a.alloc([128, 512], F32R, "fqn")
    t_t2 = a.alloc([128, 512], F32, "ft2")
    row_mx = a.alloc([2, 512], F32, "frmx")
    row_rs = a.alloc([2, 512], F32R, "frrs")
    t_sc = [a.alloc([128, 384], F32, f"fsc{i}") for i in range(2)]
    PBUF = [a.alloc([128, 384], BF16, f"fp{i}") for i in range(16)]
    sums = a.alloc([128, 16], F32, "fsm")
    rsums = a.alloc([128, 16], F32, "frs")
    negmx = a.alloc([128, 2], F32, "fnm")
    t_pn = [a.alloc([128, 384], BF16, f"fpn{i}") for i in range(2)]
    t_ptb = [a.alloc([128, 128], BF16, f"fpt{i}") for i in range(6)]
    t_vb = a.alloc([128, 512], BF16, "fvb")
    woring = [a.alloc([128, 1024], BF16, f"fwo{i}") for i in range(2)]

    for half in range(2 if not skip_attn else 0):
        h0 = half * 1024
        if half == 1:
            for mc in range(8):
                p.cp(KBUF[mc][:, 0:256], KBUF[mc][:, 1024:1280], eng="P")
            for j in range(2):
                p.cp(VBUF[j], VBUF[8 + j], eng="P")
        def evac_qk(mc, s, ps):
            is_q = mc < 8
            hc = mc % 8
            t0 = h0 + s * 512
            scol = ctx.bc("qs" if is_q else "ks", li)
            cost = ctx.l0_qcos if is_q else ctx.l0_kcos
            sint = ctx.l0_qsin if is_q else ctx.l0_ksin
            p.act(t_sq, ps, AF.Square)
            ps_ss = ctx.bank[4][0:2, 0:512]
            p.mm(ps_ss, ctx.bo2, t_sq, start=True, stop=True)
            p.ts(row_mx, ps_ss, 1e-24, ALU.max)
            p.act(row_mx, row_mx, AF.Ln)
            p.act(row_rs, row_mx, AF.Exp, scale=ctx.cc(-0.5))
            ps_bc = ctx.bank[5]
            p.mm(ps_bc, ctx.bc2, row_rs, start=True, stop=True)
            p.act(t_qe, ps, AF.Identity, scale=scol)
            p.tt(t_qn, t_qe, ps_bc, ALU.mult)
            ps_rot = ctx.bank[6]
            p.mm(ps_rot, ctx.rl0, t_qn, start=True, stop=True)
            dst = (QBUF[hc][:, s * 512:(s + 1) * 512] if is_q else
                   KBUF[hc][:, 256 + s * 512: 256 + (s + 1) * 512])
            p.tt(t_qe, t_qn, cost[:, t0:t0 + 512], ALU.mult)
            p.tt(t_t2, ps_rot, sint[:, t0:t0 + 512], ALU.mult)
            p.tt(dst, t_qe.bitcast(F32), t_t2, ALU.add)

        def evac_v(mc, s, ps):
            vc = mc - 16
            p.act(t_vb, ps, AF.Identity)
            for j in range(4):
                pst = ctx.bank[7].bitcast(BF16)[:, 0:128]
                p.transp(pst, t_vb[:, j * 128:(j + 1) * 128], ctx.ident_bf)
                p.cp(VBUF[2 + s * 4 + j][:, vc * 128:(vc + 1) * 128], pst,
                     eng="S")

        pend = None
        for mc in range(24):
            slab = wring[mc % 2]
            src = Wqkv.rearrange("(kc kp) m -> kp kc m", kp=128)[
                :, :, mc * 128:(mc + 1) * 128]
            p.dma("DG", slab, src)
            for s in range(2):
                t0 = h0 + s * 512
                ps = ctx.bank[(mc * 2 + s) % 4]
                for k in range(NCH):
                    p.mm(ps, slab[:, k * 128:(k + 1) * 128],
                         ctx.xn(k, t0, 512, shift=(k >= 4)),
                         start=(k == 0), stop=(k == NCH - 1))
                if pend is not None:
                    pmc, pst_, pps = pend
                    (evac_qk if pmc < 16 else evac_v)(pmc, pst_, pps)
                pend = (mc, s, ps)
        if pend is not None:
            pmc, pst_, pps = pend
            (evac_qk if pmc < 16 else evac_v)(pmc, pst_, pps)
        for w in range(4 * half, 4 * half + 4):
            for qt2 in range(2):
                qg0 = w * 256 + qt2 * 128
                kbase = max(0, qg0 - 256)
                width = 384 - (kbase - (qg0 - 256))
                kcol = kbase - (h0 - 256)
                mask = ctx.mask_l0[:, 384 - width:384]
                qcol = w * 256 + qt2 * 128 - h0
                for h in range(16):
                    hc, pb = h // 2, (h % 2) * 64
                    ps_sc = ctx.bank[(h % 2)][:, 0:width]
                    p.mm(ps_sc, QBUF[hc][pb:pb + 64, qcol:qcol + 128],
                         KBUF[hc][pb:pb + 64, kcol:kcol + width],
                         start=True, stop=True)
                    sc = t_sc[h % 2][:, 0:width]
                    p.stt(sc, ps_sc, 8.0, mask, ALU.mult, ALU.add)
                    p.red(negmx[:, h % 2:h % 2 + 1], sc, ALU.max, negate=True)
                    p.act(PBUF[h][:, 0:width], sc, AF.Exp,
                          bias=negmx[:, h % 2:h % 2 + 1],
                          accum=sums[:, h:h + 1])
                p.act(rsums, sums, AF.Ln)
                p.act(rsums, rsums, AF.Exp, scale=ctx.cc(-1.0))
                nkc = width // 128
                for h in range(16):
                    hc, pb = h // 2, (h % 2) * 64
                    pn = t_pn[h % 2][:, 0:width]
                    p.ts(pn, PBUF[h][:, 0:width], rsums[:, h:h + 1], ALU.mult)
                    ptb = ctx.bank[2 + h % 2].bitcast(BF16)
                    for kc in range(nkc):
                        p.transp(ptb[:, kc * 128:(kc + 1) * 128],
                                 pn[:, kc * 128:(kc + 1) * 128], ctx.ident_bf)
                    for kc in range(nkc):
                        p.cp(t_ptb[3 * (h % 2) + kc],
                             ptb[:, kc * 128:(kc + 1) * 128])
                    ps_o = ctx.bank[4 + h % 2][0:64, 0:128]
                    vi0 = kcol // 128
                    for kc in range(nkc):
                        p.mm(ps_o, VBUF[vi0 + kc][:, h * 64:(h + 1) * 64],
                             t_ptb[3 * (h % 2) + kc],
                             start=(kc == 0), stop=(kc == nkc - 1))
                    p.act(OBUF[hc][pb:pb + 64, qcol:qcol + 128], ps_o, AF.Identity)
        for s in range(2):
            for oc in range(8):
                slab = woring[oc % 2]
                p.dma("DG", slab, Wo[oc * 128:(oc + 1) * 128, :])
                for m in range(8):
                    psw = ctx.bank[m]
                    p.mm(psw, slab[:, m * 128:(m + 1) * 128],
                         OBUF[oc][:, s * 512:(s + 1) * 512],
                         start=(oc == 0), stop=(oc == 7))
                    if oc == 7:
                        t0 = h0 + s * 512
                        p.tt(ctx.xn(m, t0, 512), ctx.xn(m, t0, 512), psw,
                             ALU.add)
    p.barrier()

    if FFN_ONLY is not None and not FFN_ONLY:
        return
    a.reset()
    XN2 = [a.alloc([128, T], BF16, f"x2{i}") for i in range(8)]
    H = [a.alloc([128, 1024], BF16, f"hh{i}") for i in range(22)]
    mark0 = a.mark()

    sq_sl = a.alloc([128, 512], F32R, "fsq2")
    t_n = a.alloc([128, 512], F32, "ftn")
    row_mu = a.alloc([1, 512], F32R, "fmu")
    row_rs2 = a.alloc([1, 512], F32R, "frs2")
    row_t = a.alloc([1, 512], F32, "frt")
    for s in range(4):
        t0 = s * 512
        ps_sum = ctx.bank[0][0:1, 0:512]
        ps_ssq = ctx.bank[1][0:1, 0:512]
        for k in range(NCH):
            src = ctx.xn(k, t0, 512, shift=(k >= 4))
            p.act(sq_sl, src, AF.Square)
            p.mm(ps_sum, ctx.ones_col, src, start=(k == 0), stop=(k == NCH - 1))
            p.mm(ps_ssq, ctx.ones_col, sq_sl, start=(k == 0), stop=(k == NCH - 1))
        p.ts(row_mu, ps_sum, 1.0 / D, ALU.mult)
        p.tt(row_t, row_mu, row_mu, ALU.mult)
        p.ts(row_t, row_t, -1.0, ALU.mult)
        p.stt(row_t, ps_ssq, 1.0 / D, row_t, ALU.mult, ALU.add)
        p.act(row_t, row_t, AF.Ln, bias=ctx.cc(1e-5))
        p.act(row_rs2, row_t, AF.Exp, scale=ctx.cc(-0.5))
        ps_bmu = ctx.bank[2]
        ps_brs = ctx.bank[3]
        p.mm(ps_bmu, ctx.ones1, row_mu, start=True, stop=True)
        p.mm(ps_brs, ctx.ones1, row_rs2, start=True, stop=True)
        for k in range(NCH):
            src = ctx.xn(k, t0, 512, shift=(k >= 4))
            p.tt(t_n, src, ps_bmu, ALU.subtract)
            p.tt(t_n, t_n, ps_brs, ALU.mult)
            p.act(XN2[k][:, t0:t0 + 512], t_n, AF.Identity,
                  scale=ctx.bc("ln_g", li * 8 + k), bias=ctx.bc("ln_b", li * 8 + k))
    p.barrier()

    for half in range(2):
        h0 = half * 1024
        a.reset_to(mark0)
        wa_ring = [a.alloc([128, 1024], BF16, f"fwa{i}") for i in range(2)]
        wg_ring = [a.alloc([128, 1024], BF16, f"fwg{i}") for i in range(2)]
        t_g = a.alloc([128, 512], F32, f"ftg")
        for j in range(22):
            aw = 128 if j < 21 else 42
            wa = wa_ring[j % 2]
            wg = wg_ring[j % 2]
            p.dma("DG", wa[:, 0:8 * aw],
                  W1.rearrange("(kc kp) m -> kp kc m", kp=128)[
                      :, :, j * 128:j * 128 + aw])
            p.dma("DG", wg[:, 0:8 * aw],
                  W1.rearrange("(kc kp) m -> kp kc m", kp=128)[
                      :, :, 2730 + j * 128:2730 + j * 128 + aw])
            for s in range(2):
                t0 = h0 + s * 512
                psA = ctx.bank[0 + (s % 2) * 2][0:aw, :]
                psB = ctx.bank[1 + (s % 2) * 2][0:aw, :]
                for k in range(NCH):
                    p.mm(psA, wa[:, k * aw:(k + 1) * aw], XN2[k][:, t0:t0 + 512],
                         start=(k == 0), stop=(k == NCH - 1))
                for k in range(NCH):
                    p.mm(psB, wg[:, k * aw:(k + 1) * aw], XN2[k][:, t0:t0 + 512],
                         start=(k == 0), stop=(k == NCH - 1))
                p.act(t_g[0:aw, :], psB, AF.Gelu,
                      bias=ctx.bc("fb1", li * 44 + 22 + j))
                p.stt(H[j][0:aw, s * 512:(s + 1) * 512], psA,
                      ctx.bc("fb1", li * 44 + j), t_g[0:aw, :],
                      ALU.add, ALU.mult)
        p.barrier()
        a.reset_to(mark0)
        w2all = [a.alloc([128, 512], BF16, f"f2w{i}") for i in range(22)]
        for mh in range(2):
            for ic in range(22):
                ah = 128 if ic < 21 else 42
                p.dma("DG", w2all[ic][0:ah, :],
                      W2[ic * 128:ic * 128 + ah, mh * 512:(mh + 1) * 512])
            for mi in range(4):
                m = mh * 4 + mi
                for s in range(2):
                    t0 = h0 + s * 512
                    psw = ctx.bank[4 + (mi * 2 + s) % 4]
                    for ic in range(22):
                        ah = 128 if ic < 21 else 42
                        p.mm(psw, w2all[ic][0:ah, mi * 128:(mi + 1) * 128],
                             H[ic][0:ah, s * 512:(s + 1) * 512],
                             start=(ic == 0), stop=(ic == 21))
                    p.stt(ctx.xn(m, t0, 512), psw, ctx.bc("fb2", li * 8 + m),
                          ctx.xn(m, t0, 512), ALU.add, ALU.add)
        p.barrier()



def emit_l1_consts(ctx):
    p = ctx.p
    for name, ap in (("l1_cos", ctx.l1_cos), ("l1_sin", ctx.l1_sin),
                     ("mask_l1", ctx.mask_l1)):
        p.dma("DQ", ap, ctx.dconst[name], ch="consts")
    p.dma("DG", ctx.rl1, ctx.dconst["rl1"], ch="gconsts")
    p.dma("DQ", ctx.gateb, ctx.din["gateb_tile"], ch="consts")
    p.barrier()


def emit_moe(ctx):
    p = ctx.p
    a = ctx.arena
    a.reset()
    COMB = a.alloc([128, 256], F32, "mcb")
    CF = a.alloc([16, T], BF16, "mcf")
    gslab = a.alloc([128, 128], F32R, "mgw")
    gmark = a.mark()
    LG = a.alloc([128, 256], F32, "mlg")
    EX = a.alloc([128, 256], F32, "mex")
    ORIG = a.alloc([128, 256], F32, "mor")
    SC = a.alloc([128, 256], F32, "msc")
    CNT = a.alloc([128, 256], F32, "mcn")
    TMP = a.alloc([128, 256], F32, "mtp")
    MX = a.alloc([128, 16], F32, "mmx")
    SM = a.alloc([128, 16], F32, "msm")
    RE = a.alloc([128, 16], F32, "mre")
    GS = a.alloc([128, 64], F32, "mgs")
    GT = a.alloc([128, 64], F32, "mgt")
    KEEP = a.alloc([128, 64], F32, "mkp")
    be2sb = ctx.be2sb
    bs2row = ctx.bs2row

    gwT = ctx.din["gate_wT"]
    p.dma("DG", gslab, gwT.rearrange("(kc kp) e -> kp kc e", kp=128))
    p.dma("DG", be2sb, ctx.din["L1_ffn_be2"])
    p.dma("DG", bs2row, ctx.din["L1_ffn_bs2"].unsqueeze(0))

    for tc in range(16):
        ps_g = ctx.bank[tc % 2][:, 0:16]
        for k in range(NCH):
            p.mm(ps_g, ctx.xn(k, tc * 128, 128), gslab[:, k * 16:(k + 1) * 16],
                 start=(k == 0), stop=(k == NCH - 1))
        p.cp(LG[:, tc * 16:(tc + 1) * 16], ps_g)
    lg3 = LG.rearrange("p (a b) -> p a b", b=16)
    ex3 = EX.rearrange("p (a b) -> p a b", b=16)
    or3 = ORIG.rearrange("p (a b) -> p a b", b=16)
    sc3 = SC.rearrange("p (a b) -> p a b", b=16)
    p.red(MX, lg3, ALU.max)
    mx3 = MX.rearrange("p (a b) -> p a b", b=1).broadcast_to([128, 16, 16])
    p.tt(ex3, lg3, mx3, ALU.subtract)
    p.act(EX, EX, AF.Exp)
    p.red(SM, ex3, ALU.add)
    p.act(SM, SM, AF.Ln)
    p.act(RE, SM, AF.Exp, scale=ctx.cc(-1.0))
    re3 = RE.rearrange("p (a b) -> p a b", b=1).broadcast_to([128, 16, 16])
    p.tt(or3, ex3, re3, ALU.mult)
    gb3 = ctx.gateb.rearrange("p (a b) -> p a b", a=1).broadcast_to([128, 16, 16])
    p.tt(sc3, or3, gb3, ALU.add)
    sc4 = SC.rearrange("p (a g e) -> p (a g) e", g=4, e=4)
    gs_first = True
    for i in range(4):
        for j in range(i + 1, 4):
            dst = GS if gs_first else GT
            p.tt(dst, sc4[:, :, i:i + 1].squeeze(2), sc4[:, :, j:j + 1].squeeze(2),
                 ALU.add)
            if not gs_first:
                p.tt(GS, GS, GT, ALU.max)
            gs_first = False
    gs3 = GS.rearrange("p (a g) -> p a g", g=4)
    p.memset(KEEP, 0.0)
    for gp in range(4):
        col = gs3[:, :, gp:gp + 1].broadcast_to([128, 16, 4])
        k3 = KEEP.rearrange("p (a g) -> p a g", g=4)
        p.tt(TMP[:, 0:64].rearrange("p (a g) -> p a g", g=4), gs3, col, ALU.is_lt)
        p.tt(KEEP, KEEP, TMP[:, 0:64], ALU.add)
        if gp < 3:
            sl = gs3[:, :, gp + 1:4]
            colp = gs3[:, :, gp:gp + 1].broadcast_to([128, 16, 3 - gp])
            p.tt(TMP[:, 0:64].rearrange("p (a g) -> p a g", g=4)[:, :, gp + 1:4],
                 sl, colp, ALU.is_equal)
            p.tt(k3[:, :, gp + 1:4], k3[:, :, gp + 1:4],
                 TMP[:, 0:64].rearrange("p (a g) -> p a g", g=4)[:, :, gp + 1:4],
                 ALU.add)
    p.ts(KEEP, KEEP, 2.0, ALU.is_lt)
    p.cp(TMP.rearrange("p (x e) -> p x e", e=4),
         KEEP.unsqueeze(2).broadcast_to([128, 64, 4]))
    T2 = a.alloc([128, 256], F32, "mt2a")
    p.tt(T2, SC, TMP, ALU.mult)
    p.ts(SC, TMP, 1e30, ALU.mult, -1e30, ALU.add)
    p.tt(SC, T2, SC, ALU.add)
    p.memset(CNT, 0.0)
    m3 = SC.rearrange("p (a b) -> p a b", b=16)
    c3 = CNT.rearrange("p (a b) -> p a b", b=16)
    t23 = T2.rearrange("p (a b) -> p a b", b=16)
    for ep in range(16):
        col = m3[:, :, ep:ep + 1].broadcast_to([128, 16, 16])
        p.tt(t23, m3, col, ALU.is_lt)
        p.tt(CNT, CNT, T2, ALU.add)
        if ep < 15:
            sl = m3[:, :, ep + 1:16]
            colp = m3[:, :, ep:ep + 1].broadcast_to([128, 16, 15 - ep])
            p.tt(t23[:, :, ep + 1:16], sl, colp, ALU.is_equal)
            p.tt(c3[:, :, ep + 1:16], c3[:, :, ep + 1:16], t23[:, :, ep + 1:16],
                 ALU.add)
    p.ts(CNT, CNT, 2.0, ALU.is_lt)
    p.tt(COMB, ORIG, CNT, ALU.mult)
    if DEBUG_MOE:
        p.dma("DQ", ctx.y_out[0:128, 0:256], COMB)
        p.dma("DQ", ctx.y_out[128:256, 0:256], ORIG)
        p.dma("DQ", ctx.y_out[256:384, 0:256], SC)
        p.dma("DQ", ctx.y_out[384:512, 0:64], GS)
        p.dma("DQ", ctx.y_out[512:640, 0:64], KEEP)
        p.dma("DQ", ctx.y_out[640:768, 0:256], CNT)
        p.dma("DQ", ctx.y_out[768:896, 0:256], TMP)
    for tc in range(16):
        ps_t = ctx.bank[2][0:16, 0:128]
        cslice = COMB[:, tc * 16:(tc + 1) * 16]
        p.transp(ps_t, cslice.bitcast(F32), ctx.ident_f32)
        p.cp(CF[0:16, tc * 128:(tc + 1) * 128], ps_t)
    p.barrier()
    a.reset_to(gmark)
    y_acc = [a.alloc([128, 1024], F32, f"my{i}") for i in range(4)]
    H = [a.alloc([128, 512], BF16, f"mh{i}") for i in range(4)]
    HS = [a.alloc([128, 512], BF16, f"ms{i}") for i in range(8)]
    w1r = [a.alloc([128, 1024], F32R, f"mw1{i}") for i in range(2)]
    w3r = [a.alloc([128, 1024], F32R, f"mw3{i}") for i in range(2)]
    w2r = [a.alloc([128, 1024], BF16, f"mw2{i}") for i in range(1)]
    t_si = a.alloc([128, 512], F32, "msi")

    for s in range(4):
        t0 = s * 512
        for tt in range(4):
            for dh in range(2):
                ps_y = ctx.bank[tt * 2 + dh]
                p.mm(ps_y, CF[0:16, t0 + tt * 128:t0 + (tt + 1) * 128],
                     be2sb[0:16, dh * 512:(dh + 1) * 512], start=True, stop=True)
                p.cp(y_acc[tt][:, dh * 512:(dh + 1) * 512], ps_y)
        for e in range(16):
            for ic in range(4):
                ps1 = ctx.bank[0 + (ic % 2) * 2]
                ps3 = ctx.bank[1 + (ic % 2) * 2]
                s1 = w1r[ic % 2]
                s3 = w3r[ic % 2]
                p.dma("DG", s1, ctx.din["L1_ffn_We1"][e].rearrange(
                    "(kc kp2) i -> kp2 kc i", kp2=128)[
                    :, :, ic * 128:(ic + 1) * 128])
                p.dma("DG", s3, ctx.din["L1_ffn_We3"][e].rearrange(
                    "(kc kp2) i -> kp2 kc i", kp2=128)[
                    :, :, ic * 128:(ic + 1) * 128])
                for k in range(NCH):
                    p.mm(ps1, s1[:, k * 128:(k + 1) * 128],
                         ctx.xn(k, t0, 512), start=(k == 0), stop=(k == 7))
                for k in range(NCH):
                    p.mm(ps3, s3[:, k * 128:(k + 1) * 128],
                         ctx.xn(k, t0, 512), start=(k == 0), stop=(k == 7))
                p.act(t_si, ps1, AF.Silu, bias=ctx.bc("be1", e * 4 + ic))
                p.stt(H[ic], ps3, ctx.bc("be3", e * 4 + ic), t_si,
                      ALU.add, ALU.mult)
            for ic in range(4):
                p.dma("DG", w2r[0],
                      ctx.din["L1_ffn_We2"][e][ic * 128:(ic + 1) * 128, :])
                for tt in range(4):
                    for dh in range(2):
                        ps_y = ctx.bank[tt * 2 + dh]
                        p.mm(ps_y, H[ic][:, tt * 128:(tt + 1) * 128],
                             w2r[0][:, dh * 512:(dh + 1) * 512],
                             start=(ic == 0), stop=(ic == 3))
            for tt in range(4):
                comb_col = COMB[:, (s * 4 + tt) * 16 + e:(s * 4 + tt) * 16 + e + 1]
                for dh in range(2):
                    ps_y = ctx.bank[tt * 2 + dh]
                    p.stt(y_acc[tt][:, dh * 512:(dh + 1) * 512], ps_y, comb_col,
                          y_acc[tt][:, dh * 512:(dh + 1) * 512],
                          ALU.mult, ALU.add)
        for icg in range(8):
            ps1 = ctx.bank[0 + (icg % 2) * 2]
            ps3 = ctx.bank[1 + (icg % 2) * 2]
            s1 = w1r[icg % 2]
            s3 = w3r[icg % 2]
            p.dma("DG", s1, ctx.din["L1_ffn_ws1"].rearrange(
                "(kc kp2) i -> kp2 kc i", kp2=128)[
                :, :, icg * 128:(icg + 1) * 128])
            p.dma("DG", s3, ctx.din["L1_ffn_ws3"].rearrange(
                "(kc kp2) i -> kp2 kc i", kp2=128)[
                :, :, icg * 128:(icg + 1) * 128])
            for k in range(NCH):
                p.mm(ps1, s1[:, k * 128:(k + 1) * 128],
                     ctx.xn(k, t0, 512), start=(k == 0), stop=(k == 7))
            for k in range(NCH):
                p.mm(ps3, s3[:, k * 128:(k + 1) * 128],
                     ctx.xn(k, t0, 512), start=(k == 0), stop=(k == 7))
            p.act(t_si, ps1, AF.Silu, bias=ctx.bc("bs1", icg))
            p.stt(HS[icg], ps3, ctx.bc("bs3", icg), t_si, ALU.add, ALU.mult)
        for icg in range(8):
            p.dma("DG", w2r[0],
                  ctx.din["L1_ffn_ws2"][icg * 128:(icg + 1) * 128, :])
            for tt in range(4):
                for dh in range(2):
                    ps_y = ctx.bank[tt * 2 + dh]
                    p.mm(ps_y, HS[icg][:, tt * 128:(tt + 1) * 128],
                         w2r[0][:, dh * 512:(dh + 1) * 512],
                         start=(icg == 0), stop=False)
        for tt in range(4):
            for dh in range(2):
                ps_y = ctx.bank[tt * 2 + dh]
                p.mm(ps_y, ctx.ones1, bs2row[0:1, dh * 512:(dh + 1) * 512],
                     start=False, stop=True)
                p.tt(y_acc[tt][:, dh * 512:(dh + 1) * 512],
                     y_acc[tt][:, dh * 512:(dh + 1) * 512], ps_y, ALU.add)
        for tt in range(4):
            for dc in range(8):
                ps_tr = ctx.bank[4 + dc % 2]
                p.transp(ps_tr[:, 0:128], y_acc[tt][:, dc * 128:(dc + 1) * 128],
                         ctx.ident_f32)
                xs = ctx.xres(dc, t0 + tt * 128, 128)
                p.tt(xs, xs, ps_tr[:, 0:128], ALU.add)
    p.barrier()


def emit_l1_attn(ctx):
    p = ctx.p
    a = ctx.arena
    a.reset()
    KB = [a.alloc([128, T], BF16, f"lk{i}") for i in range(2)]
    VB = [a.alloc([128, 256], BF16, f"lv{i}") for i in range(16)]
    QB = [a.alloc([128, 128], BF16, f"lq{i}") for i in range(8)]
    wring = [a.alloc([128, 1024], F32R, f"lw{i}") for i in range(2)]
    t_ke = a.alloc([128, 512], F32R, "lke")
    t_kr = a.alloc([128, 512], F32R, "lkr")
    row_mx = a.alloc([2, 512], F32, "lrm")
    row_rs = a.alloc([2, 512], F32R, "lrr")
    t_sc = a.alloc([128, T], F32, "lsc")
    t_pu = a.alloc([128, T], BF16, "lpu")
    t_pt = [a.alloc([128, 128], BF16, f"lpt{i}") for i in range(4)]
    OB = [a.alloc([128, 128], BF16, f"lo{i}") for i in range(8)]
    wpring = [a.alloc([128, 1024], BF16, f"lwp{i}") for i in range(2)]
    sums = a.alloc([128, 8], F32, "lsm")
    rsum = a.alloc([128, 2], F32, "lrs")
    negmx = a.alloc([128, 2], F32, "lnm")
    w_attn = ctx.din["L1_attn_w_attn"]
    w_proj = ctx.din["L1_attn_w_proj"]

    def rope_rms(ps, dst, t0, n):
        ke = t_ke[:, 0:n]
        kr = t_kr[:, 0:n]
        p.act(ke, ps, AF.Identity)
        ps_rot = ctx.bank[6]
        p.mm(ps_rot[:, 0:n], ctx.rl1, ke, start=True, stop=True)
        p.tt(kr, ke, ctx.l1_cos[:, t0:t0 + n], ALU.mult)
        p.tt(ke, ps_rot[:, 0:n], ctx.l1_sin[:, t0:t0 + n], ALU.mult)
        p.tt(kr, kr, ke, ALU.add)
        p.act(ke, kr, AF.Square)
        ps_ss = ctx.bank[7][0:2, 0:n]
        p.mm(ps_ss, ctx.bo2, ke, start=True, stop=True)
        p.act(row_mx[:, 0:n], ps_ss, AF.Ln, scale=ctx.cc(1.0 / 64),
              bias=ctx.cc(1e-6))
        p.act(row_rs[:, 0:n], row_mx[:, 0:n], AF.Exp, scale=ctx.cc(-0.5))
        ps_bc = ctx.bank[5]
        p.mm(ps_bc[:, 0:n], ctx.bc2, row_rs[:, 0:n], start=True, stop=True)
        p.tt(dst, kr, ps_bc[:, 0:n], ALU.mult)

    for mc in range(4):
        slab = wring[mc % 2]
        src = w_attn.rearrange("(kc kp) m -> kp kc m", kp=128)[
            :, :, 1024 + mc * 128:1024 + (mc + 1) * 128]
        p.dma("DG", slab, src)
        for s in range(4):
            t0 = s * 512
            ps = ctx.bank[(mc * 4 + s) % 4]
            for k in range(NCH):
                p.mm(ps, slab[:, k * 128:(k + 1) * 128], ctx.xn(k, t0, 512),
                     start=(k == 0), stop=(k == NCH - 1))
            if mc < 2:
                rope_rms(ps, KB[mc][:, t0:t0 + 512], t0, 512)
            else:
                vc = mc - 2
                p.act(t_pu[:, 0:512], ps, AF.Identity)
                for j in range(4):
                    pst = ctx.bank[7].bitcast(BF16)[:, 256:384]
                    p.transp(pst, t_pu[:, j * 128:(j + 1) * 128], ctx.ident_bf)
                    p.cp(VB[s * 4 + j][:, vc * 128:(vc + 1) * 128], pst, eng="S")

    wv = w_attn.rearrange("(kc kp) m -> kp kc m", kp=128)
    for qt in range(16):
        q0 = qt * 128
        for mc in range(8):
            A = (mc % 4) + 8 * (mc // 4)
            B = A + 4
            slab = wring[mc % 2]
            sv = slab.rearrange("p (kc m) -> p kc m", m=128)
            p.dma("DG", sv[:, :, 0:64], wv[:, :, A * 64:(A + 1) * 64], ch=slab.tensor.name)
            p.dma("DG", sv[:, :, 64:128], wv[:, :, B * 64:(B + 1) * 64], ch=slab.tensor.name + "b")
            ps = ctx.bank[mc % 4]
            for k in range(NCH):
                p.mm(ps[:, 0:128], slab[:, k * 128:(k + 1) * 128],
                     ctx.xn(k, q0, 128), start=(k == 0), stop=(k == NCH - 1))
            rope_rms(ps[:, 0:128], QB[mc], q0, 128)
        kspan = (qt + 1) * 128
        nblk = cdiv(kspan, 512)
        for h in range(16):
            hc, pb = (h % 4) + 4 * (h // 8), ((h // 4) % 2) * 64
            g = h // 4
            kc_, kpb = g // 2, (g % 2) * 64
            for b in range(nblk):
                bw = min(512, kspan - b * 512)
                ps_sc = ctx.bank[h % 2]
                p.mm(ps_sc[:, 0:bw], QB[hc][pb:pb + 64, :],
                     KB[kc_][kpb:kpb + 64, b * 512:b * 512 + bw],
                     start=True, stop=True)
                if b == nblk - 1:
                    sc = t_sc[:, b * 512:b * 512 + bw]
                    p.stt(sc, ps_sc[:, 0:bw], 0.125,
                          ctx.mask_l1[:, 512 - bw:512], ALU.mult, ALU.add)
                else:
                    p.ts(t_sc[:, b * 512:b * 512 + bw], ps_sc[:, 0:bw], 0.125,
                         ALU.mult)
            p.red(negmx[:, 0:1], t_sc[:, 0:kspan], ALU.max, negate=True)
            p.act(t_pu[:, 0:kspan], t_sc[:, 0:kspan], AF.Exp,
                  bias=negmx[:, 0:1], accum=sums[:, 0:1])
            p.act(sums[:, 4:5], sums[:, 0:1], AF.Ln)
            p.act(rsum[:, 0:1], sums[:, 4:5], AF.Exp, scale=ctx.cc(-1.0))
            p.ts(t_pu[:, 0:kspan], t_pu[:, 0:kspan], rsum[:, 0:1], ALU.mult)
            ps_o = ctx.bank[4 + h % 2][0:64, 0:128]
            ptb = ctx.bank[2 + h % 2].bitcast(BF16)
            for kc0 in range(0, qt + 1, 8):
                ke = min(kc0 + 8, qt + 1)
                for kc in range(kc0, ke):
                    p.transp(ptb[:, (kc % 8) * 128:(kc % 8 + 1) * 128],
                             t_pu[:, kc * 128:(kc + 1) * 128], ctx.ident_bf)
                for kc in range(kc0, ke):
                    p.cp(t_pt[kc % 4], ptb[:, (kc % 8) * 128:(kc % 8 + 1) * 128])
                    p.mm(ps_o, VB[kc][:, g * 64:(g + 1) * 64], t_pt[kc % 4],
                         start=(kc == 0), stop=(kc == qt))
            p.act(OB[hc][pb:pb + 64, :], ps_o, AF.Identity)
        for oc in range(8):
            A = (oc % 4) + 8 * (oc // 4)
            B = A + 4
            slab = wpring[oc % 2]
            p.dma("DG", slab[0:64, :], w_proj[A * 64:(A + 1) * 64, :],
                  ch=slab.tensor.name)
            p.dma("DG", slab[64:128, :], w_proj[B * 64:(B + 1) * 64, :],
                  ch=slab.tensor.name + "b")
            for m in range(8):
                psw = ctx.bank[m][:, 0:128]
                p.mm(psw, slab[:, m * 128:(m + 1) * 128], OB[oc],
                     start=(oc == 0), stop=(oc == 7))
                if oc == 7:
                    xs = ctx.xres(m, q0, 128)
                    p.tt(xs, xs, psw, ALU.add)
    p.barrier()
